# version 14
# baseline (speedup 1.0000x reference)
import numpy as np
import ml_dtypes
from contextlib import ExitStack

import concourse.bass as bass
import concourse.mybir as mybir
from concourse.bass_utils import run_bass_kernel_spmd

F32 = mybir.dt.float32
BF16 = mybir.dt.bfloat16
I32 = mybir.dt.int32
U32 = mybir.dt.uint32
U8 = mybir.dt.uint8
AF = mybir.ActivationFunctionType
ALU = mybir.AluOpType
AX = mybir.AxisListType

NCORE = 8
T = 2048
NCH = 16
HID = 2048
H = 16
KVH = 4
DH = 128
KVD = 512
EPS_ATTN = 1e-6
LN_EPS = 1e-5
BIG = 1.0e30
CBIG = 65536.0
NT = 64
DS = 32
GRID = 16
bfdt = ml_dtypes.bfloat16


class KB:
    def __init__(self, nc, ctx, prefix=""):
        self.nc = nc
        self.ctx = ctx
        self.prefix = prefix
        self.sems = {}

    def sem(self, *names):
        for name in names:
            h = self.ctx.enter_context(self.nc.semaphore(self.prefix + name))
            self.sems[name] = [h, 0]

    def inc(self, name, inst, k=1):
        s = self.sems[name]
        inst.then_inc(s[0], k)
        s[1] += k
        return s[1]

    def n(self, name):
        return self.sems[name][1]

    def h(self, name):
        return self.sems[name][0]

    def wait(self, eng, name, val=None):
        eng.wait_ge(self.sems[name][0], self.sems[name][1] if val is None else val)


def build_l12(dbg=False):
    nc = bass.Bass("TRN2", target_bir_lowering=False)
    dt = lambda n, s, d=F32, k="ExternalInput": nc.dram_tensor(n, s, d, kind=k)
    dbg_kv = dt("dbg_kv", [128, 1032], F32, "ExternalOutput").ap() if dbg else None
    dbg_rr = dt("dbg_rr", [1, 4], I32, "ExternalOutput").ap() if dbg else None
    dbg_at = dt("dbg_at", [128, HID], BF16, "ExternalOutput").ap() if dbg else None
    dbg_a0 = dt("dbg_a0", [128, HID], F32, "ExternalOutput").ap() if dbg else None
    dbg_sq = dt("dbg_sq", [128, HID], BF16, "ExternalOutput").ap() if dbg else None

    x0_d = dt("x0", [T, HID]).ap()
    wqt_d = dt("wqt", [HID, HID]).ap()
    wkt_d = dt("wkt", [HID, KVD]).ap()
    wvt_d = dt("wvt", [HID, KVD]).ap()
    bq_d = dt("bq_pc", [128, 16]).ap()
    bk_d = dt("bk_row", [1, KVD]).ap()
    bv_d = dt("bv_row", [1, KVD]).ap()
    pr_d = dt("pr1", [1, 1], I32).ap()
    nn_d = dt("nn1", [1, 1], I32).ap()
    wot_d = dt("wot", [HID, HID]).ap()
    gam_d = dt("gam_row", [1, HID]).ap()
    bet_d = dt("bet_row", [1, HID]).ap()
    slp_d = dt("slopes2", [NT, DS * GRID]).ap()
    gb2_d = dt("gb2", [NT, 64]).ap()

    x1_o = dt("x1_out", [T, HID], F32, "ExternalOutput").ap()
    oh_o = dt("oh_out", [T, NT], BF16, "ExternalOutput").ap()
    xsl_o = dt("xsl_out", [T, DS], F32, "ExternalOutput").ap()
    mv_o = dt("mv_out", [128, 32], F32, "ExternalOutput").ap()
    st_o = dt("st_out", [128, 1], F32, "ExternalOutput").ap()

    sq_scr = nc.dram_tensor("sq_scr", [128, H * T], BF16, kind="Internal")
    ss_scr = nc.dram_tensor("ss_scr", [NT, DS], F32, kind="Internal")
    KVS = nc.dram_tensor("KVS", [128, 2 * 516], F32, addr_space="Shared")
    FLG = nc.dram_tensor("FLG", [1, 16], I32, addr_space="Shared")
    sq_d = sq_scr.ap()

    identb_d = nc.inline_tensor(np.eye(128, dtype=bfdt), "identb_c").ap()
    identf_d = nc.inline_tensor(np.eye(128, dtype=np.float32), "identf_c").ap()
    iot64_d = nc.inline_tensor(np.arange(64, dtype=np.float32)[None, :], "iot64_c").ap()
    iot8_d = nc.inline_tensor(np.arange(8, dtype=np.float32)[None, :], "iot8_c").ap()

    NJ = NCH + 8
    MAXIT = 2000000

    ctx = ExitStack()
    with ctx:
        E = ctx.enter_context
        kba = KB(nc, ctx, "a_")
        kbb = KB(nc, ctx, "b_")
        kba.sem("ldc", "ld0", "ld1", "gld", "gw0", "gw1", "gw2", "gw3", "vc", "sc",
                "p1v", "p2t", "p2s", "kvs", "kbr", "els", "txp0", "txp1",
                "p4w0", "p4w1", "p4t", "p4s", "p4v", "sqs",
                "gin", "kvx", "flw", "plf", "kvp")
        kbb.sem("ldc", "vc", "sc", "scs", "ssb", "kvb_ok",
                "sql0", "sql1", "sql2", "sql3",
                "nmt", "nrs", "dv", "p5t", "atx0", "atx1", "p7s", "p7t", "p7v",
                "x0l0", "x0l1", "x1s0", "x1s1",
                "p8v", "p8s", "p8m", "p8n", "p8t1", "p8e1", "p8t2", "p8e2",
                "p8t3", "p8r", "fin", "stf")
        kb = kba

        sb = lambda n, s, d: E(nc.sbuf_tensor(n, s, d, side="left"))
        bqc = sb("bqc", [128, 16], F32)
        nbqc = sb("nbqc", [128, 16], F32)
        kvloc = sb("kvloc", [128, 516], F32)
        stat6 = sb("stat6", [128, 24], F32)
        mvec = sb("mvec", [128, 2], F32)
        rvec = sb("rvec", [128, 1], F32)
        prb = sb("prb", [1, 1], I32)
        nnb = sb("nnb", [1, 1], I32)
        flb = sb("flb", [1, 2], I32)

        rg = lambda n: E(nc.gpsimd.register(n))
        rpar, rnn, rexp, rcnd, rrem = rg("rpar"), rg("rnn"), rg("rexp"), rg("rcnd"), rg("rrem")
        fv0, fv1, fe0, fe1, rall, rd_ = rg("fv0"), rg("fv1"), rg("fe0"), rg("fe1"), rg("rall"), rg("rd_")

        block = E(nc.Block())

        s_xin = ExitStack()
        xin_fm = s_xin.enter_context(nc.sbuf_tensor("xin_fm", [128, NCH * T], BF16, side="left", is_transpose_dest=True))
        c12 = ExitStack()
        E2 = c12.enter_context
        x0t = [E2(nc.sbuf_tensor(f"ax0t{j}", [128, HID], F32, side="right")) for j in range(2)]
        xint = [E2(nc.sbuf_tensor(f"xint{j}", [128, HID], BF16, side="right")) for j in range(2)]
        wkt_sb = E2(nc.sbuf_tensor("wkt_sb", [128, NCH * KVD], BF16, side="right"))
        wvt_sb = E2(nc.sbuf_tensor("wvt_sb", [128, NCH * KVD], BF16, side="right"))
        bkrep = E2(nc.sbuf_tensor("bkrep", [128, KVD], F32, side="right"))
        bvrep = E2(nc.sbuf_tensor("bvrep", [128, KVD], F32, side="right"))
        kbf = E2(nc.sbuf_tensor("kbf", [128, KVD], F32, side="right"))
        t1f = E2(nc.sbuf_tensor("t1f", [128, KVD], F32, side="right"))
        t2f = E2(nc.sbuf_tensor("t2f", [128, KVD], F32, side="right"))
        t3f = E2(nc.sbuf_tensor("t3f", [128, KVD], F32, side="right"))
        sk_t = E2(nc.sbuf_tensor("sk_t", [128, KVD], BF16, side="right"))
        vext = E2(nc.sbuf_tensor("vext", [128, 516], BF16, side="right"))
        pk = E2(nc.psum_tensor("pk", [128, KVD], F32))
        pv = E2(nc.psum_tensor("pv", [128, KVD], F32))
        pkv = [E2(nc.psum_tensor(f"pkv{j}", [128, 129], F32)) for j in range(4)]

        NLD0 = 3

        @block.sync
        def _(sy):
            kb.inc("ldc", sy.dma_start(out=bqc[:, :], in_=bq_d), 16)
            kb.inc("ldc", sy.dma_start(out=bkrep[:, :], in_=bk_d.broadcast_to([128, KVD])), 16)
            kb.inc("ldc", sy.dma_start(out=bvrep[:, :], in_=bv_d.broadcast_to([128, KVD])), 16)
            assert kb.n("ldc") == 16 * NLD0
            for i in range(NCH + 1):
                if i < NCH:
                    if i >= 2:
                        sy.wait_ge(kb.h("p1v"), 2 * (i - 1))
                    kb.inc("ld%d" % (i % 2), sy.dma_start(out=x0t[i % 2][:, :],
                                                          in_=x0_d[i * 128:(i + 1) * 128, :]), 16)
                if i >= 1:
                    j0 = i - 1
                    sy.wait_ge(kb.h("p1v"), 2 * (j0 + 1))
                    kb.inc("txp%d" % (j0 % 2), sy.dma_start(
                        out=xin_fm[:, :].rearrange("p (j t) -> p j t", t=T)[:, :, j0 * 128:(j0 + 1) * 128],
                        in_=xint[j0 % 2][:, :].rearrange("p (j d) -> p j d", d=128),
                        transpose=True), 16)

        @block.gpsimd
        def _(g):
            g.dma_start(out=prb[:, :], in_=pr_d).then_inc(kba.h("gin"), 16)
            g.dma_start(out=nnb[:, :], in_=nn_d).then_inc(kba.h("gin"), 16)
            g.wait_ge(kba.h("gin"), 32)
            g.reg_load(rpar, prb[0:1, 0:1])
            g.reg_load(rnn, nnb[0:1, 0:1])
            for q4 in range(4):
                kb.inc("gw%d" % q4, g.dma_start(
                    out=wkt_sb[:, q4 * 4 * KVD:(q4 + 1) * 4 * KVD].rearrange("p (dc o) -> p dc o", o=KVD),
                    in_=wkt_d.rearrange("(dc p) o -> p dc o", p=128)[:, q4 * 4:(q4 + 1) * 4, :]), 16)
                kb.inc("gw%d" % q4, g.dma_start(
                    out=wvt_sb[:, q4 * 4 * KVD:(q4 + 1) * 4 * KVD].rearrange("p (dc o) -> p dc o", o=KVD),
                    in_=wvt_d.rearrange("(dc p) o -> p dc o", p=128)[:, q4 * 4:(q4 + 1) * 4, :]), 16)

        @block.vector
        def _(ve):
            def skstuff(i):
                ve.wait_ge(kb.h("p2t"), 3 * i + 1)
                kb.inc("kbr", ve.tensor_add(kbf[:, :], pk[:, :], bkrep[:, :]))
                ve.wait_ge(kb.h("p2t"), 3 * i + 2)
                inst = ve.tensor_add(
                    vext[:, :].rearrange("p (a b) -> p a b", b=129)[:, :, 0:128],
                    pv[:, :].rearrange("p (a b) -> p a b", b=128),
                    bvrep[:, :].rearrange("p (a b) -> p a b", b=128))
                kb.inc("kvs", inst)
                ve.wait_ge(kb.h("els"), 2 * (i + 1))
                inst = ve.tensor_add(sk_t[:, :], t2f[:, :], t3f[:, :])
                kb.inc("kvs", inst)

            ve.wait_ge(kb.h("ldc"), 16 * NLD0)
            kb.inc("vc", ve.tensor_scalar(out=nbqc[:, :], in0=bqc[:, :], scalar1=-1.0,
                                          scalar2=None, op0=ALU.mult))
            kb.inc("vc", ve.memset(vext[:, :], 1.0))
            for i in range(NCH):
                ve.wait_ge(kb.h("ld%d" % (i % 2)), 16 * (i // 2 + 1))
                xc = x0t[i % 2]
                kb.inc("vc", ve.bn_stats(stat6[:, 0:6], xc[:, 0:512]))
                kb.inc("vc", ve.bn_stats(stat6[:, 6:12], xc[:, 512:1024]))
                kb.inc("vc", ve.bn_stats(stat6[:, 12:18], xc[:, 1024:1536]))
                kb.inc("vc", ve.bn_stats(stat6[:, 18:24], xc[:, 1536:2048]))
                kb.wait(ve, "vc")
                kb.inc("vc", ve.bn_aggr(mvec[:, :], stat6[:, :]))
                kb.wait(ve, "vc")
                kb.inc("p1v", ve.tensor_scalar(out=rvec[:, :], in0=mvec[:, 1:2],
                                               scalar1=LN_EPS, scalar2=None, op0=ALU.add))
                ve.wait_ge(kb.h("p2s"), i + 1)
                if i >= 2:
                    ve.wait_ge(kb.h("txp%d" % (i % 2)), 16 * (i // 2))
                kb.inc("p1v", ve.tensor_scalar(
                    out=xint[i % 2][:, :], in0=xc[:, :], scalar1=mvec[:, 0:1],
                    scalar2=rvec[:, 0:1], op0=ALU.subtract, op1=ALU.mult))
                if i >= 1:
                    skstuff(i - 1)
            skstuff(NCH - 1)
            ve.wait_ge(kb.h("p2t"), 3 * NCH)
            kb.inc("vc", ve.tensor_copy(kvloc[:, 0:129], pkv[0][:, :]))
            kb.inc("vc", ve.tensor_copy(kvloc[:, 129:258], pkv[1][:, :]))
            kb.inc("vc", ve.tensor_copy(kvloc[:, 258:387], pkv[2][:, :]))
            inst = ve.tensor_copy(kvloc[:, 387:516], pkv[3][:, :])
            kb.inc("kvs", inst)

        @block.scalar
        def _(sc):
            def elu_k(i):
                sc.wait_ge(kb.h("kbr"), i + 1)
                kb.inc("sc", sc.activation(t1f[:, :], kbf[:, :], AF.Relu, scale=-1.0))
                kb.inc("els", sc.activation(t3f[:, :], kbf[:, :], AF.Relu))
                kb.wait(sc, "sc")
                kb.inc("els", sc.activation(t2f[:, :], t1f[:, :], AF.Exp, scale=-1.0))

            for i in range(NCH):
                sc.wait_ge(kb.h("p1v"), 2 * i + 1)
                kb.inc("sc", sc.activation(rvec[:, :], rvec[:, :], AF.Ln))
                kb.wait(sc, "sc")
                kb.inc("p2s", sc.activation(rvec[:, :], rvec[:, :], AF.Exp, scale=-0.5))
                if i >= 1:
                    elu_k(i - 1)
            elu_k(NCH - 1)

        @block.tensor
        def _(te):
            te.wait_ge(kb.h("ldc"), 16 * NLD0)
            for i in range(NCH):
                te.wait_ge(kb.h("txp%d" % (i % 2)), 16 * (i // 2 + 1))
                if i >= 1:
                    te.wait_ge(kb.h("kbr"), i)
                    te.wait_ge(kb.h("kvs"), 2 * i - 1)
                for dc in range(NCH):
                    if i == 0:
                        te.wait_ge(kb.h("gw%d" % (dc // 4)), 32)
                    inst = te.matmul(pk[:, :], xin_fm[:, dc * T + i * 128:dc * T + (i + 1) * 128],
                                     wkt_sb[:, dc * KVD:(dc + 1) * KVD],
                                     start=(dc == 0), stop=(dc == NCH - 1))
                kb.inc("p2t", inst)
                for dc in range(NCH):
                    inst = te.matmul(pv[:, :], xin_fm[:, dc * T + i * 128:dc * T + (i + 1) * 128],
                                     wvt_sb[:, dc * KVD:(dc + 1) * KVD],
                                     start=(dc == 0), stop=(dc == NCH - 1))
                kb.inc("p2t", inst)
                te.wait_ge(kb.h("kvs"), 2 * i + 2)
                for kv4 in range(KVH):
                    inst = te.matmul(pkv[kv4][:, :],
                                     sk_t[:, kv4 * 128:(kv4 + 1) * 128],
                                     vext[:, kv4 * 129:(kv4 + 1) * 129],
                                     start=(i == 0), stop=(i == NCH - 1))
                kb.inc("p2t", inst)

        c12.close()
        c4 = ExitStack()
        E4 = c4.enter_context
        sq_fm = E4(nc.sbuf_tensor("sq_fm", [128, NCH * T], BF16, side="right"))
        wqb = [E4(nc.sbuf_tensor(f"wqb{j}", [128, NCH * 128], BF16, side="right")) for j in range(2)]
        qa1 = E4(nc.sbuf_tensor("qa1", [128, 512], F32, side="right"))
        qa2 = E4(nc.sbuf_tensor("qa2", [128, 512], F32, side="right"))
        qa3 = E4(nc.sbuf_tensor("qa3", [128, 512], F32, side="right"))
        pq = [E4(nc.psum_tensor(f"pq{j}", [128, 512], F32)) for j in range(2)]

        @block.gpsimd
        def _(g):
            g.wait_ge(kb.h("kvs"), 2 * NCH + 1)
            g.wait_ge(kb.h("p2t"), 3 * NCH)
            g.wait_ge(kb.h("txp0"), 16 * (NCH // 2))
            g.wait_ge(kb.h("txp1"), 16 * (NCH // 2))
            with g.If_eq(rpar, 0):
                g.dma_start(out=KVS.ap()[:, 0:516], in_=kvloc[:, :]).then_inc(kba.h("kvx"), 16)
            with g.Else():
                g.dma_start(out=KVS.ap()[:, 516:1032], in_=kvloc[:, :]).then_inc(kba.h("kvx"), 16)
            g.wait_ge(kba.h("kvx"), 16)
            with g.If_eq(rpar, 0):
                g.dma_start(out=FLG.ap()[0:1, 0:1], in_=nnb[0:1, 0:1]).then_inc(kba.h("flw"), 16)
            with g.Else():
                g.dma_start(out=FLG.ap()[0:1, 1:2], in_=nnb[0:1, 0:1]).then_inc(kba.h("flw"), 16)
            g.wait_ge(kba.h("flw"), 16)
            for oc in range(16):
                if oc >= 2:
                    g.wait_ge(kb.h("p4t"), 4 * (oc - 1))
                kb.inc("p4w%d" % (oc % 2), g.dma_start(
                    out=wqb[oc % 2][:, :].rearrange("p (dc o) -> p dc o", o=128),
                    in_=wqt_d.rearrange("(dc p) o -> p dc o", p=128)[:, :, oc * 128:(oc + 1) * 128]), 16)

        @block.tensor
        def _(te):
            for oc in range(16):
                te.wait_ge(kb.h("p4w%d" % (oc % 2)), 16 * (oc // 2 + 1))
                for tl in range(4):
                    n4 = oc * 4 + tl
                    if n4 >= 2:
                        te.wait_ge(kb.h("p4v"), n4 - 1)
                    for dc in range(NCH):
                        inst = te.matmul(pq[n4 % 2][:, :],
                                         wqb[oc % 2][:, dc * 128:(dc + 1) * 128],
                                         xin_fm[:, dc * T + tl * 512:dc * T + (tl + 1) * 512],
                                         start=(dc == 0), stop=(dc == NCH - 1))
                    kb.inc("p4t", inst)

        @block.scalar
        def _(sc):
            for oc in range(16):
                for tl in range(4):
                    n4 = oc * 4 + tl
                    sc.wait_ge(kb.h("p4t"), n4 + 1)
                    kb.inc("sc", sc.activation(qa1[:, :], pq[n4 % 2][:, :], AF.Relu,
                                               scale=-1.0, bias=nbqc[:, oc:oc + 1]))
                    kb.inc("p4s", sc.activation(qa3[:, :], pq[n4 % 2][:, :], AF.Relu,
                                                bias=bqc[:, oc:oc + 1]))
                    kb.wait(sc, "sc")
                    kb.inc("p4s", sc.activation(qa2[:, :], qa1[:, :], AF.Exp, scale=-1.0))

        @block.vector
        def _(ve):
            for oc in range(16):
                for tl in range(4):
                    n4 = oc * 4 + tl
                    ve.wait_ge(kb.h("p4s"), 2 * n4 + 2)
                    kb.inc("p4v", ve.tensor_add(
                        sq_fm[:, oc * T + tl * 512:oc * T + (tl + 1) * 512],
                        qa2[:, :], qa3[:, :]))

        @block.sync
        def _(sy):
            for oc in range(16):
                sy.wait_ge(kb.h("p4v"), 4 * (oc + 1))
                kb.inc("sqs", sy.dma_start(
                    out=sq_d.rearrange("p (i ht) -> p i ht", ht=H * 128)[:, :, oc * 128:(oc + 1) * 128],
                    in_=sq_fm[:, oc * T:(oc + 1) * T].rearrange("p (i t) -> p i t", t=128)), 16)
            sy.wait_ge(kb.h("sqs"), 16 * 16)

        s_xin.close()
        c4.close()

        kb = kbb

        sb = lambda n, s, d: E(nc.sbuf_tensor(n, s, d, side="left"))
        identb = sb("identb", [128, 128], BF16)
        identf = sb("identf", [128, 128], F32)
        kvb = sb("kvb", [128, 516], BF16)
        slp_sb = sb("slp_sb", [64, DS * GRID], F32)
        ssd = sb("ssd", [64, DS], F32)
        gb2s = sb("gb2s", [64, 64], F32)
        nrc = sb("nrc", [128, 32], F32)
        epsb = sb("epsb", [128, 1], F32)
        vz = sb("vz", [128, 1], F32)
        r2v = sb("r2v", [128, 1], F32)
        Rv = sb("Rv", [128, 1], F32)
        mv7 = sb("mv7", [128, 2], F32)
        rv7 = sb("rv7", [128, 1], F32)
        mr7n = sb("mr7n", [128, 1], F32)
        m1v1 = sb("m1v1", [128, 32], F32)
        statrun = sb("statrun", [128, 1], F32)
        tokmn = sb("tokmn", [128, 1], F32)
        tokmx = sb("tokmx", [128, 1], F32)
        csc = sb("csc", [128, 8], F32)
        mx8 = sb("mx8", [128, 8], F32)
        idxu = sb("idxu", [128, 8], U32)
        idxf = sb("idxf", [128, 8], F32)
        ocoh = sb("ocoh", [128, 8], F32)
        mskd = sb("mskd", [128, 64], F32)
        msk8 = sb("msk8", [128, 64], U8)
        tsc = sb("tsc", [128, 64], F32)
        bigt = sb("bigt", [128, 64], F32)
        iot64 = sb("iot64", [128, 64], F32)
        iot8 = sb("iot8", [128, 8], F32)
        selmm = [sb(f"selmm{j}", [128, 128], F32) for j in range(2)]
        colst = sb("colst", [128, 1], F32)
        ohTs = sb("ohTs", [64, 128], F32)
        gamsel = sb("gamsel", [128, 64], F32)
        wsel = sb("wsel", [128, 8 * DS], F32)
        xslst = sb("xslst", [128, NCH * DS], F32)
        ohstore = sb("ohstore", [128, NCH * 64], BF16)
        kvpart = sb("kvpart", [128, 516], F32)
        sqc = [sb(f"sqc{j}", [128, H * 128], BF16) for j in range(4)]
        attnr = [sb(f"attnr{j}", [128, HID], BF16) for j in range(2)]

        rb = lambda n, s, d: E(nc.sbuf_tensor(n, s, d, side="right"))
        wot_sb = rb("wot_sb", [128, NCH * HID], BF16)
        afm = [E(nc.sbuf_tensor(f"afm{j}", [128, HID], BF16, side="right",
                                is_transpose_dest=True)) for j in range(2)]
        x0t = [rb(f"x0t{j}", [128, HID], F32) for j in range(2)]
        aout = rb("aout", [128, HID], F32)
        x1t = [rb(f"x1t{j}", [128, HID], F32) for j in range(2)]
        w_t = rb("w_t", [128, HID], F32)
        scr = rb("scr", [128, HID], F32)
        y_t = rb("y_t", [128, HID], BF16)
        gsrep = rb("gsrep", [128, HID], F32)
        nbsrep = rb("nbsrep", [128, HID], F32)
        ohf32 = rb("ohf32", [128, 8 * 64], F32)

        pq5 = [E(nc.psum_tensor(f"pq5_{j}", [128, 512], F32)) for j in range(4)]
        paux = E(nc.psum_tensor("paux", [128, 512], F32))
        po7 = [E(nc.psum_tensor(f"po7_{j}", [128, 512], F32)) for j in range(2)]
        ptr8 = E(nc.psum_tensor("ptr8", [128, 128], F32))

        NLD = 8

        @block.sync
        def _(sy):
            for t_, d_ in [(identb, identb_d), (identf, identf_d),
                           (slp_sb, slp_d), (gb2s, gb2_d)]:
                kb.inc("ldc", sy.dma_start(out=t_[:, :], in_=d_), 16)
            kb.inc("ldc", sy.dma_start(out=iot64[:, :], in_=iot64_d.broadcast_to([128, 64])), 16)
            kb.inc("ldc", sy.dma_start(out=iot8[:, :], in_=iot8_d.broadcast_to([128, 8])), 16)
            kb.inc("ldc", sy.dma_start(out=w_t[:, :], in_=gam_d.broadcast_to([128, HID])), 16)
            kb.inc("ldc", sy.dma_start(out=scr[:, :], in_=bet_d.broadcast_to([128, HID])), 16)
            kb.inc("sql0", sy.dma_start(out=sqc[0][:, :], in_=sq_d[:, 0:2048]), 16)
            kb.inc("sql1", sy.dma_start(out=sqc[1][:, :], in_=sq_d[:, 2048:4096]), 16)
            assert kb.n("ldc") == 16 * NLD
            for j in range(NCH + 1):
                if j < NCH:
                    if j >= 2:
                        sy.wait_ge(kb.h("p8v"), 4 * (j - 2) + 3)
                    kb.inc("x0l%d" % (j % 2), sy.dma_start(
                        out=x0t[j % 2][:, :], in_=x0_d[j * 128:(j + 1) * 128, :]), 16)
                    sy.wait_ge(kb.h("dv"), 16 * (j + 1))
                    if j >= 2:
                        sy.wait_ge(kb.h("p7t"), 4 * (j - 1))
                    kb.inc("atx%d" % (j % 2), sy.dma_start(
                        out=afm[j % 2][:, :].rearrange("p (a b) -> p a b", b=128),
                        in_=attnr[j % 2][:, :].rearrange("p (a b) -> p a b", b=128),
                        transpose=True), 16)
                    if j + 2 < NCH:
                        if j >= 2:
                            sy.wait_ge(kb.h("p5t"), j - 1)
                        kb.inc("sql%d" % ((j + 2) % 4), sy.dma_start(
                            out=sqc[(j + 2) % 4][:, :],
                            in_=sq_d[:, (j + 2) * 2048:(j + 3) * 2048]), 16)
                if j >= 1:
                    c = j - 1
                    sy.wait_ge(kb.h("p8v"), 4 * c + 3)
                    kb.inc("x1s%d" % (c % 2), sy.dma_start(
                        out=x1_o[c * 128:(c + 1) * 128, :], in_=x1t[c % 2][:, :]), 16)

        @block.gpsimd
        def _(g):
            g.wait_ge(kba.h("sqs"), 256)
            g.wait_ge(kba.h("p4t"), 64)
            g.wait_ge(kba.h("p4v"), 64)
            g.reg_mov(rexp, 16)
            g.reg_mov(rcnd, 1)
            g.reg_mov(rrem, MAXIT)
            with g.While(rcnd):
                g.dma_start(out=flb[0:1, 0:2], in_=FLG.ap()[0:1, 0:2]).then_inc(kba.h("plf"), 16)
                g.wait_ge(kba.h("plf"), rexp)
                g.reg_add(rexp, rexp, 16)
                g.reg_load(fv0, flb[0:1, 0:1])
                g.reg_load(fv1, flb[0:1, 1:2])
                g.reg_alu(fe0, fv0, rnn, ALU.is_equal)
                g.reg_alu(fe1, fv1, rnn, ALU.is_equal)
                g.reg_alu(rall, fe0, fe1, ALU.mult)
                g.reg_sub(rd_, 1, rall)
                g.reg_sub(rrem, rrem, 1)
                g.reg_alu(rcnd, rd_, rrem, ALU.mult)
            with g.If_eq(rpar, 0):
                g.dma_start(out=kvpart[:, :], in_=KVS.ap()[:, 516:1032]).then_inc(kba.h("kvp"), 16)
            with g.Else():
                g.dma_start(out=kvpart[:, :], in_=KVS.ap()[:, 0:516]).then_inc(kba.h("kvp"), 16)
            if dbg:
                g.reg_save(dbg_rr[0:1, 0:1], rrem)
                g.reg_save(dbg_rr[0:1, 1:2], fv0)
                g.reg_save(dbg_rr[0:1, 2:3], fv1)
                g.reg_save(dbg_rr[0:1, 3:4], rnn)
                g.wait_ge(kba.h("kvp"), 16)
                g.dma_start(out=dbg_kv[:, 0:516], in_=kvloc[:, :]).then_inc(kba.h("gin"), 16)
                g.dma_start(out=dbg_kv[:, 516:1032], in_=kvpart[:, :]).then_inc(kba.h("gin"), 16)
                g.wait_ge(kba.h("gin"), 64)
            kb.inc("p7s", g.dma_start(
                out=wot_sb[:, :].rearrange("p (ac h) -> p ac h", h=HID),
                in_=wot_d.rearrange("(ac p) h -> p ac h", p=128)), 16)
            g.wait_ge(kb.h("scs"), 1)
            kb.inc("ssb", g.dma_start(out=ss_scr.ap(), in_=ssd[:, :]), 16)
            g.wait_ge(kb.h("ssb"), 16)
            kb.inc("ssb", g.dma_start(
                out=y_t[:, :],
                in_=ss_scr.ap().rearrange("a b -> (a b)")[None, :].broadcast_to([128, HID])), 16)
            if dbg:
                g.wait_ge(kb.h("sql1"), 32)
                g.dma_start(out=dbg_sq, in_=sqc[1][:, :]).then_inc(kba.h("gin"), 16)
                g.wait_ge(kb.h("dv"), 16 * 6)
                g.dma_start(out=dbg_at, in_=attnr[1][:, :]).then_inc(kba.h("gin"), 16)
                g.wait_ge(kb.h("p7v"), 4 * 5 + 4)
                g.dma_start(out=dbg_a0, in_=aout[:, :]).then_inc(kba.h("gin"), 16)
                g.wait_ge(kba.h("gin"), 112)

        @block.vector
        def _(ve):
            ve.wait_ge(kb.h("ldc"), 16 * NLD)
            ve.wait_ge(kba.h("kvp"), 16)
            ve.wait_ge(kba.h("sqs"), 256)
            kb.inc("vc", ve.tensor_add(kvloc[:, :], kvloc[:, :], kvpart[:, :]))
            kb.wait(ve, "vc")
            kb.inc("kvb_ok", ve.tensor_copy(kvb[:, :], kvloc[:, :]))
            kb.inc("vc", ve.tensor_reduce(
                out=ssd[:, :], in_=slp_sb[:, :].rearrange("p (a b) -> p a b", b=GRID),
                axis=AX.X, op=ALU.add))
            kb.wait(ve, "vc")
            kb.inc("vc", ve.tensor_scalar(out=ssd[:, :], in0=ssd[:, :],
                                          scalar1=1.0 / GRID, scalar2=None, op0=ALU.mult))
            kb.inc("vc", ve.memset(epsb[:, :], EPS_ATTN))
            kb.inc("vc", ve.memset(bigt[:, :], BIG))
            kb.inc("vc", ve.memset(statrun[:, :], BIG))
            ve.wait_ge(kb.h("ssb"), 32)
            kb.inc("vc", ve.tensor_mul(gsrep[:, :], w_t[:, :], y_t[:, :]))
            inst = ve.scalar_tensor_tensor(
                out=nbsrep[:, :], in0=scr[:, :], scalar=-1.0, in1=y_t[:, :],
                op0=ALU.mult, op1=ALU.mult)
            kb.inc("vc", inst)
            ve.wait_ge(kb.h("vc"), kb.n("vc"))

        @block.scalar
        def _(sc):
            sc.wait_ge(kb.h("vc"), 6)
            kb.inc("scs", sc.activation(ssd[:, :], ssd[:, :], AF.Sign))

        @block.tensor
        def _(te):
            te.wait_ge(kb.h("ldc"), 16 * NLD)
            te.wait_ge(kb.h("kvb_ok"), 1)
            te.wait_ge(kb.h("p7s"), 16)
            for j in range(NJ):
                if 0 <= j - 2 < NCH:
                    c = j - 2
                    te.wait_ge(kb.h("atx%d" % (c % 2)), 16 * (c // 2 + 1))
                    for hc in range(4):
                        n7 = c * 4 + hc
                        if n7 >= 2:
                            te.wait_ge(kb.h("p7v"), n7 - 1)
                        for ac in range(NCH):
                            inst = te.matmul(po7[n7 % 2][:, :],
                                             afm[c % 2][:, ac * 128:(ac + 1) * 128],
                                             wot_sb[:, ac * HID + hc * 512:ac * HID + (hc + 1) * 512],
                                             start=(ac == 0), stop=(ac == NCH - 1))
                        kb.inc("p7t", inst)
                if j < NCH:
                    te.wait_ge(kb.h("sql%d" % (j % 4)), 16 * (j // 4 + 1))
                    if j >= 2:
                        te.wait_ge(kb.h("nrs"), j - 1)
                    for h_ in range(16):
                        inst = te.matmul(
                            paux[:, (j % 2) * 16 + h_:(j % 2) * 16 + h_ + 1],
                            sqc[j % 4][:, h_ * 128:(h_ + 1) * 128],
                            kvb[:, (h_ // 4) * 129 + 128:(h_ // 4) * 129 + 129],
                            start=True, stop=True)
                    kb.inc("nmt", inst)
                    if j >= 1:
                        te.wait_ge(kb.h("dv"), 16 * j)
                    for h_ in range(16):
                        inst = te.matmul(
                            pq5[h_ % 4][:, (h_ // 4) * 128:(h_ // 4) * 128 + 128],
                            sqc[j % 4][:, h_ * 128:(h_ + 1) * 128],
                            kvb[:, (h_ // 4) * 129:(h_ // 4) * 129 + 128],
                            start=True, stop=True)
                    kb.inc("p5t", inst)
                if 0 <= j - 4 < NCH:
                    c = j - 4
                    te.wait_ge(kb.h("p8m"), c + 1)
                    if c >= 1:
                        te.wait_ge(kb.h("p8e1"), c)
                    inst = te.transpose(paux[0:64, 96:224],
                                        ohf32[:, (c % 8) * 64:(c % 8) * 64 + 64],
                                        identf[:, :])
                    kb.inc("p8t1", inst)
                if 0 <= j - 5 < NCH:
                    c = j - 5
                    te.wait_ge(kb.h("p8e1"), c + 1)
                    if c >= 1:
                        te.wait_ge(kb.h("p8e2"), c)
                    inst = te.matmul(paux[:, 32:96], ohTs[:, :], gb2s[:, :],
                                     start=True, stop=True)
                    kb.inc("p8t2", inst)
                if 0 <= j - 7 < NCH:
                    c = j - 7
                    te.wait_ge(kb.h("p8n"), c + 1)
                    if c >= 1:
                        te.wait_ge(kb.h("p8r"), c)
                    inst = te.transpose(ptr8[:, :], selmm[c % 2][:, :], identf[:, :])
                    kb.inc("p8t3", inst)

        @block.scalar
        def _(sc):
            for j in range(NJ):
                if 0 <= j - 1 < NCH:
                    c = j - 1
                    if c >= 1:
                        sc.wait_ge(kb.h("p8v"), 4 * (c - 1) + 3)
                    for hc in range(4):
                        sc.wait_ge(kb.h("p7t"), 4 * c + hc + 1)
                        kb.inc("p7v", sc.activation(aout[:, hc * 512:(hc + 1) * 512],
                                                    po7[(c * 4 + hc) % 2][:, :], AF.Copy))
                    sc.wait_ge(kb.h("p8v"), 4 * c + 1)
                    kb.inc("sc", sc.activation(rv7[:, :], rv7[:, :], AF.Ln))
                    kb.wait(sc, "sc")
                    kb.inc("p8s", sc.activation(rv7[:, :], rv7[:, :], AF.Exp, scale=-0.5))
                if j < NCH:
                    sc.wait_ge(kb.h("nmt"), j + 1)
                    sc.wait_ge(kb.h("vc"), 6)
                    nsl = nrc[:, (j % 2) * 16:(j % 2) * 16 + 16]
                    kb.inc("sc", sc.activation(nsl, paux[:, (j % 2) * 16:(j % 2) * 16 + 16],
                                               AF.Ln, bias=epsb[:, 0:1]))
                    kb.wait(sc, "sc")
                    kb.inc("nrs", sc.activation(nsl, nsl, AF.Exp, scale=-1.0))
                    sc.wait_ge(kb.h("nrs"), j + 1)
                    sc.wait_ge(kb.h("p5t"), j + 1)
                    if j >= 2:
                        sc.wait_ge(kb.h("atx%d" % (j % 2)), 16 * (j // 2))
                    for h_ in range(16):
                        kb.inc("dv", sc.activation(
                            attnr[j % 2][:, h_ * 128:(h_ + 1) * 128],
                            pq5[h_ % 4][:, (h_ // 4) * 128:(h_ // 4) * 128 + 128],
                            AF.Copy, scale=nrc[:, (j % 2) * 16 + h_:(j % 2) * 16 + h_ + 1]))
                if 0 <= j - 1 < NCH:
                    c = j - 1
                    sc.wait_ge(kb.h("p8v"), 4 * c + 4)
                    kb.inc("sc", sc.activation(rvec[:, :], rvec[:, :], AF.Ln))
                    kb.wait(sc, "sc")
                    kb.inc("p8s", sc.activation(rvec[:, :], rvec[:, :], AF.Exp, scale=-0.5))
                    kb.inc("sc", sc.activation(r2v[:, :], vz[:, :], AF.Ln))
                    kb.wait(sc, "sc")
                    kb.inc("p8s", sc.activation(r2v[:, :], r2v[:, :], AF.Exp, scale=-0.5))
                if 0 <= j - 4 < NCH:
                    c = j - 4
                    sc.wait_ge(kb.h("p8t1"), c + 1)
                    if c >= 1:
                        sc.wait_ge(kb.h("p8t2"), c)
                    kb.inc("p8e1", sc.activation(ohTs[:, :], paux[0:64, 96:224], AF.Copy))
                if 0 <= j - 5 < NCH:
                    c = j - 5
                    sc.wait_ge(kb.h("p8t2"), c + 1)
                    if c >= 1:
                        sc.wait_ge(kb.h("p8n"), c)
                    kb.inc("p8e2", sc.activation(gamsel[:, :], paux[:, 32:96], AF.Copy))

        @block.vector
        def _(ve):
            for j in range(NJ):
                if 0 <= j - 7 < NCH:
                    c = j - 7
                    ve.wait_ge(kb.h("p8t3"), c + 1)
                    kb.inc("vc", ve.tensor_reduce(out=colst[:, :], in_=ptr8[:, :],
                                                  axis=AX.X, op=ALU.min))
                    kb.wait(ve, "vc")
                    if c >= 1:
                        ve.wait_ge(kb.h("p8r"), c)
                    inst = ve.tensor_tensor(statrun[:, :], statrun[:, :], colst[:, :], ALU.min)
                    kb.inc("p8r", inst)
                    if c == NCH - 1:
                        kb.wait(ve, "vc")
                        ve.wait_ge(kb.h("p8r"), NCH)
                        kb.inc("fin", ve.tensor_copy(ohstore[:, 0:1], ohstore[:, 0:1]))
                        kb.inc("fin", ve.tensor_copy(xslst[:, 0:1], xslst[:, 0:1]))
                        kb.inc("fin", ve.tensor_copy(m1v1[:, 0:1], m1v1[:, 0:1]))
                        kb.inc("fin", ve.tensor_copy(statrun[:, 0:1], statrun[:, 0:1]))
                if 0 <= j - 5 < NCH:
                    c = j - 5
                    ve.wait_ge(kb.h("p8e2"), c + 1)
                    xsl = xslst[:, c * DS:(c + 1) * DS]
                    kb.inc("vc", ve.tensor_mul(xsl, wsel[:, (c % 8) * DS:(c % 8 + 1) * DS],
                                               gamsel[:, 0:32]))
                    kb.wait(ve, "vc")
                    kb.inc("vc", ve.tensor_add(xsl, xsl, gamsel[:, 32:64]))
                    kb.wait(ve, "vc")
                    kb.inc("vc", ve.tensor_reduce(out=tokmn[:, :], in_=xsl,
                                                  axis=AX.X, op=ALU.min))
                    kb.inc("vc", ve.tensor_reduce(out=tokmx[:, :], in_=xsl,
                                                  axis=AX.X, op=ALU.max))
                    kb.wait(ve, "vc")
                    kb.inc("vc", ve.tensor_scalar(out=tokmx[:, :], in0=tokmx[:, :],
                                                  scalar1=-1.0, scalar2=None, op0=ALU.mult))
                    kb.wait(ve, "vc")
                    kb.inc("vc", ve.tensor_copy(msk8[:, :], ohstore[:, c * 64:(c + 1) * 64]))
                    kb.wait(ve, "vc")
                    if c >= 2:
                        ve.wait_ge(kb.h("p8t3"), c - 1)
                    kb.inc("vc", ve.select(selmm[c % 2][:, 0:64], msk8[:, :],
                                           tokmn[:, :].broadcast_to([128, 64]), bigt[:, :]))
                    kb.wait(ve, "vc")
                    inst = ve.select(selmm[c % 2][:, 64:128], msk8[:, :],
                                     tokmx[:, :].broadcast_to([128, 64]), bigt[:, :])
                    kb.inc("p8n", inst)
                if j < NCH:
                    ve.wait_ge(kb.h("p7v"), 4 * j + 1)
                    kb.inc("vc", ve.bn_stats(stat6[:, 0:6], aout[:, 0:512]))
                    ve.wait_ge(kb.h("p7v"), 4 * j + 2)
                    kb.inc("vc", ve.bn_stats(stat6[:, 6:12], aout[:, 512:1024]))
                    ve.wait_ge(kb.h("p7v"), 4 * j + 3)
                    kb.inc("vc", ve.bn_stats(stat6[:, 12:18], aout[:, 1024:1536]))
                    ve.wait_ge(kb.h("p7v"), 4 * (j + 1))
                    kb.inc("vc", ve.bn_stats(stat6[:, 18:24], aout[:, 1536:2048]))
                    kb.wait(ve, "vc")
                    kb.inc("vc", ve.bn_aggr(mv7[:, :], stat6[:, :]))
                    kb.wait(ve, "vc")
                    kb.inc("p8v", ve.tensor_scalar(out=rv7[:, :], in0=mv7[:, 1:2],
                                                   scalar1=LN_EPS, scalar2=None, op0=ALU.add))
                    ve.wait_ge(kb.h("p8s"), 3 * j + 1)
                    kb.inc("p8v", ve.tensor_scalar(out=mr7n[:, :], in0=mv7[:, 0:1],
                                                   scalar1=rv7[:, 0:1], scalar2=-1.0,
                                                   op0=ALU.mult, op1=ALU.mult))
                    ve.wait_ge(kb.h("p8v"), kb.n("p8v"))
                    ve.wait_ge(kb.h("x0l%d" % (j % 2)), 16 * (j // 2 + 1))
                    kb.inc("vc", ve.tensor_scalar(out=x0t[j % 2][:, :], in0=x0t[j % 2][:, :],
                                                  scalar1=mr7n[:, 0:1], scalar2=None,
                                                  op0=ALU.add))
                    kb.wait(ve, "vc")
                    if j >= 2:
                        ve.wait_ge(kb.h("x1s%d" % (j % 2)), 16 * (j // 2))
                    inst = ve.scalar_tensor_tensor(
                        out=x1t[j % 2][:, :], in0=aout[:, :], scalar=rv7[:, 0:1],
                        in1=x0t[j % 2][:, :], op0=ALU.mult, op1=ALU.add)
                    kb.inc("p8v", inst)
                    ve.wait_ge(kb.h("p8v"), 4 * j + 3)
                    x1c = x1t[j % 2]
                    kb.inc("vc", ve.bn_stats(stat6[:, 0:6], x1c[:, 0:512]))
                    kb.inc("vc", ve.bn_stats(stat6[:, 6:12], x1c[:, 512:1024]))
                    kb.inc("vc", ve.bn_stats(stat6[:, 12:18], x1c[:, 1024:1536]))
                    kb.inc("vc", ve.bn_stats(stat6[:, 18:24], x1c[:, 1536:2048]))
                    kb.wait(ve, "vc")
                    kb.inc("vc", ve.bn_aggr(mvec[:, :], stat6[:, :]))
                    kb.wait(ve, "vc")
                    kb.inc("vc", ve.tensor_copy(m1v1[:, 2 * j:2 * j + 2], mvec[:, :]))
                    kb.inc("vc", ve.tensor_scalar(out=rvec[:, :], in0=mvec[:, 1:2],
                                                  scalar1=LN_EPS, scalar2=None, op0=ALU.add))
                    kb.wait(ve, "vc")
                    kb.inc("vc", ve.reciprocal(vz[:, :], rvec[:, :]))
                    kb.wait(ve, "vc")
                    kb.inc("vc", ve.tensor_tensor(vz[:, :], vz[:, :], mvec[:, 1:2], ALU.mult))
                    kb.wait(ve, "vc")
                    inst = ve.tensor_scalar(out=vz[:, :], in0=vz[:, :],
                                            scalar1=LN_EPS, scalar2=None, op0=ALU.add)
                    kb.inc("p8v", inst)
                    ve.wait_ge(kb.h("p8s"), 3 * j + 3)
                    kb.inc("vc", ve.tensor_tensor(Rv[:, :], rvec[:, :], r2v[:, :], ALU.mult))
                    kb.wait(ve, "vc")
                    kb.inc("vc", ve.tensor_scalar(out=w_t[:, :], in0=x1c[:, :],
                                                  scalar1=mvec[:, 0:1], scalar2=Rv[:, 0:1],
                                                  op0=ALU.subtract, op1=ALU.mult))
                    kb.wait(ve, "vc")
                    kb.inc("vc", ve.tensor_mul(scr[:, :], w_t[:, :], gsrep[:, :]))
                    kb.wait(ve, "vc")
                    kb.inc("vc", ve.tensor_tensor(y_t[:, :], scr[:, :], nbsrep[:, :], ALU.is_ge))
                    kb.wait(ve, "vc")
                    kb.inc("vc", ve.tensor_reduce(
                        out=tsc[:, :], in_=y_t[:, :].rearrange("p (a b) -> p a b", b=DS),
                        axis=AX.X, op=ALU.add))
                    kb.wait(ve, "vc")
                    kb.inc("vc", ve.tensor_scalar(out=tsc[:, :], in0=tsc[:, :],
                                                  scalar1=2.0, scalar2=-32.0,
                                                  op0=ALU.mult, op1=ALU.add))
                    kb.wait(ve, "vc")
                    kb.inc("vc", ve.tensor_reduce(
                        out=csc[:, :], in_=tsc[:, :].rearrange("p (a b) -> p a b", b=8),
                        axis=AX.X, op=ALU.add))
                    kb.wait(ve, "vc")
                    kb.inc("vc", ve.max(mx8[:, :], csc[:, :]))
                    kb.wait(ve, "vc")
                    kb.inc("vc", ve.max_index(idxu[:, :], mx8[:, :], csc[:, :]))
                    kb.wait(ve, "vc")
                    kb.inc("vc", ve.tensor_copy(idxf[:, :], idxu[:, :]))
                    kb.wait(ve, "vc")
                    kb.inc("vc", ve.tensor_scalar(out=ocoh[:, :], in0=iot8[:, :],
                                                  scalar1=idxf[:, 0:1], scalar2=None,
                                                  op0=ALU.is_equal))
                    kb.wait(ve, "vc")
                    kb.inc("vc", ve.scalar_tensor_tensor(
                        out=mskd[:, :],
                        in0=ocoh[:, :, None].broadcast_to([128, 8, 8]),
                        scalar=CBIG, in1=tsc[:, :], op0=ALU.mult, op1=ALU.add))
                    kb.wait(ve, "vc")
                    kb.inc("vc", ve.max(mx8[:, :], mskd[:, :]))
                    kb.wait(ve, "vc")
                    kb.inc("vc", ve.max_index(idxu[:, :], mx8[:, :], mskd[:, :]))
                    kb.wait(ve, "vc")
                    kb.inc("vc", ve.tensor_copy(idxf[:, :], idxu[:, :]))
                    kb.wait(ve, "vc")
                    ohsl = ohstore[:, j * 64:(j + 1) * 64]
                    kb.inc("vc", ve.tensor_scalar(
                        out=ohf32[:, (j % 8) * 64:(j % 8) * 64 + 64], in0=iot64[:, :],
                        scalar1=idxf[:, 0:1], scalar2=None, op0=ALU.is_equal))
                    inst = ve.tensor_scalar(out=ohsl, in0=iot64[:, :],
                                            scalar1=idxf[:, 0:1], scalar2=None,
                                            op0=ALU.is_equal)
                    kb.inc("p8m", inst)
                    ve.wait_ge(kb.h("p8m"), j + 1)
                    kb.inc("vc", ve.tensor_mul(
                        scr[:, :].rearrange("p (a b) -> p a b", b=DS),
                        w_t[:, :].rearrange("p (a b) -> p a b", b=DS),
                        ohsl[:, :, None].broadcast_to([128, 64, DS])))
                    kb.wait(ve, "vc")
                    kb.inc("vc", ve.tensor_reduce(
                        out=wsel[:, (j % 8) * DS:(j % 8 + 1) * DS],
                        in_=scr[:, :].rearrange("p (t i) -> p i t", i=DS),
                        axis=AX.X, op=ALU.add))
                    kb.wait(ve, "vc")

        @block.sync
        def _(sy):
            sy.wait_ge(kb.h("fin"), 4)
            sy.wait_ge(kb.h("x1s0"), 16 * (NCH // 2))
            sy.wait_ge(kb.h("x1s1"), 16 * (NCH // 2))
            kb.inc("stf", sy.dma_start(out=oh_o.rearrange("(i p) t -> p i t", p=128),
                                       in_=ohstore[:, :].rearrange("p (i t) -> p i t", t=64)), 16)
            kb.inc("stf", sy.dma_start(out=xsl_o.rearrange("(i p) t -> p i t", p=128),
                                       in_=xslst[:, :].rearrange("p (i t) -> p i t", t=DS)), 16)
            kb.inc("stf", sy.dma_start(out=mv_o, in_=m1v1[:, :]), 16)
            kb.inc("stf", sy.dma_start(out=st_o, in_=statrun[:, :]), 16)
            sy.wait_ge(kb.h("stf"), kb.n("stf"))
    return nc


def build_l2():
    nc = bass.Bass("TRN2", target_bir_lowering=False)
    dt = lambda n, s, d=F32, k="ExternalInput": nc.dram_tensor(n, s, d, kind=k)

    x1_d = dt("x1", [T, HID]).ap()
    oh_d = dt("oh", [T, NT], BF16).ap()
    xsl_d = dt("xsl", [T, DS]).ap()
    mv_d = dt("mv", [128, 32]).ap()
    sta_d = dt("statall", [128, 8]).ap()
    bas_d = dt("bases2", [NT, DS * GRID]).ap()
    slp_d = dt("slopes2", [NT, DS * GRID]).ap()
    osc_d = dt("oscale", [NT, 1]).ap()
    out_d = dt("out", [T, HID], F32, "ExternalOutput").ap()
    st_scr = nc.dram_tensor("st_scr", [128, 1], F32, kind="Internal")

    identb_d = nc.inline_tensor(np.eye(128, dtype=bfdt), "identb_c").ap()
    iotg_d = nc.inline_tensor(
        np.tile(np.arange(GRID, dtype=np.float32), DS)[None, :], "iotg_c").ap()

    with ExitStack() as ctx:
        E = ctx.enter_context
        kb = KB(nc, ctx)
        kb.sem("ld", "stb", "ost0", "ost1", "vc", "sc", "tc", "te2", "q0", "q0s",
               "q1t", "q1e", "q1r", "q1f", "q1x", "q1z", "x1l0", "x1l1", "fin",
               "q1m", "zr", "zc", "q1n", "q1b")
        sbt = lambda n, s, d: E(nc.sbuf_tensor(n, s, d))

        identb = sbt("identb", [128, 128], BF16)
        iotg = sbt("iotg", [128, DS * GRID], F32)
        statall = sbt("sta_sb", [128, 8], F32)
        statc = sbt("statc", [128, 1], F32)
        tminr = sbt("tminr", [128, 64], F32)
        tmaxr = sbt("tmaxr", [128, 64], F32)
        bas = sbt("bas", [64, DS * GRID], F32)
        slp = sbt("slp", [64, DS * GRID], F32)
        osc = sbt("osc", [64, 1], F32)
        basb = sbt("basb", [64, DS * GRID], BF16)
        slpb = sbt("slpb", [64, DS * GRID], BF16)
        ohall = sbt("ohall", [128, NCH * 64], BF16)
        ohfm = sbt("ohfm", [64, T], BF16)
        xslal = sbt("xslal", [128, NCH * DS], F32)
        mval = sbt("mval", [128, 32], F32)
        x1b = [sbt(f"x1b{j}", [128, HID], F32) for j in range(2)]
        r1v = sbt("r1v", [128, 32], F32)
        fr3 = sbt("fr3", [128, 32], F32)
        z_t = [sbt(f"z_t{j}", [128, HID], F32) for j in range(2)]
        b_row = sbt("b_row", [128, DS * GRID], F32)
        s_row = sbt("s_row", [128, DS * GRID], F32)
        mnv = sbt("mnv", [128, 1], F32)
        mxv = sbt("mxv", [128, 1], F32)
        den = sbt("den", [128, 1], F32)
        u_t = sbt("u_t", [128, DS], F32)
        u16 = sbt("u16", [128, DS], F32)
        Cst = sbt("Cst", [128, DS * GRID], F32)
        gi_t = sbt("gi_t", [128, DS], F32)
        xl_t = sbt("xl_t", [128, DS], F32)
        og_t = sbt("og_t", [128, DS * GRID], BF16)
        bsel = sbt("bsel", [128, DS * GRID], F32)
        b_v = sbt("b_v", [128, DS], F32)
        s_v = sbt("s_v", [128, DS], F32)
        delta = sbt("delta", [128, DS], F32)
        sc_t = sbt("sc_t", [128, HID], F32)
        fr_t = sbt("fr_t", [128, HID], F32)
        fs6 = sbt("fs6", [128, 24], F32)
        fmv = sbt("fmv", [128, 2], F32)
        fm3 = sbt("fm3", [128, 1], F32)
        x2t = [sbt(f"x2t{j}", [128, HID], F32) for j in range(2)]
        tmp64 = sbt("tmp64", [128, 64], F32)
        pso = [E(nc.psum_tensor(f"pso{j}", [128, 512], F32)) for j in range(2)]
        ptro = E(nc.psum_tensor("ptro", [64, 128], BF16))

        block = E(nc.Block())
        NL2 = 9

        @block.sync
        def _(sy):
            for t_, d_ in [(identb, identb_d), (statall, sta_d), (bas, bas_d),
                           (slp, slp_d), (osc, osc_d), (mval, mv_d)]:
                kb.inc("ld", sy.dma_start(out=t_[:, :], in_=d_), 16)
            kb.inc("ld", sy.dma_start(out=iotg[:, :],
                                      in_=iotg_d.broadcast_to([128, DS * GRID])), 16)
            kb.inc("ld", sy.dma_start(
                out=ohall[:, :].rearrange("p (i t) -> p i t", t=64),
                in_=oh_d.rearrange("(i p) t -> p i t", p=128)), 16)
            kb.inc("ld", sy.dma_start(
                out=xslal[:, :].rearrange("p (i t) -> p i t", t=DS),
                in_=xsl_d.rearrange("(i p) t -> p i t", p=128)), 16)
            assert kb.n("ld") == 16 * NL2
            sy.wait_ge(kb.h("q0"), 1)
            kb.inc("stb", sy.dma_start(out=st_scr.ap(), in_=statc[:, :]), 16)
            sy.wait_ge(kb.h("stb"), 16)
            kb.inc("stb", sy.dma_start(
                out=tminr[:, :],
                in_=st_scr.ap().rearrange("a b -> (a b)")[None, 0:64].broadcast_to([128, 64])), 16)
            kb.inc("stb", sy.dma_start(
                out=tmaxr[:, :],
                in_=st_scr.ap().rearrange("a b -> (a b)")[None, 64:128].broadcast_to([128, 64])), 16)
            sy.wait_ge(kb.h("stb"), 48)
            sy.sem_inc(kb.h("q0s"), 1)
            kb.sems["q0s"][1] += 1
            for i in range(NCH + 2):
                if i < NCH:
                    if i >= 2:
                        sy.wait_ge(kb.h("q1x"), i - 1)
                    kb.inc("x1l%d" % (i % 2), sy.dma_start(
                        out=x1b[i % 2][:, :], in_=x1_d[i * 128:(i + 1) * 128, :]), 16)
                if i >= 2:
                    j = i - 2
                    sy.wait_ge(kb.h("q1x"), j + 1)
                    kb.inc("ost%d" % (j % 2), sy.dma_start(
                        out=out_d[j * 128:(j + 1) * 128, :], in_=x2t[j % 2][:, :]), 16)
            sy.wait_ge(kb.h("ost0"), 16 * (NCH // 2))
            sy.wait_ge(kb.h("ost1"), 16 * (NCH // 2))

        @block.vector
        def _(ve):
            ve.wait_ge(kb.h("ld"), 16 * NL2)
            inst = ve.tensor_reduce(out=statc[:, :], in_=statall[:, :],
                                    axis=AX.X, op=ALU.min)
            inst.then_inc(kb.h("q0"), 1)
            kb.sems["q0"][1] += 1
            kb.inc("vc", ve.tensor_scalar(out=bas[:, :], in0=bas[:, :],
                                          scalar1=osc[:, 0:1], scalar2=None, op0=ALU.mult))
            kb.inc("vc", ve.tensor_scalar(out=slp[:, :], in0=slp[:, :],
                                          scalar1=osc[:, 0:1], scalar2=None, op0=ALU.mult))
            kb.wait(ve, "vc")
            kb.inc("vc", ve.tensor_copy(basb[:, :], bas[:, :]))
            inst = ve.tensor_copy(slpb[:, :], slp[:, :])
            inst.then_inc(kb.h("q0"), 1)
            kb.sems["q0"][1] += 1
            ve.wait_ge(kb.h("q0s"), 1)
            inst = ve.tensor_scalar(out=tmaxr[:, :], in0=tmaxr[:, :],
                                    scalar1=-1.0, scalar2=None, op0=ALU.mult)
            inst.then_inc(kb.h("q0"), 1)
            kb.sems["q0"][1] += 1
            ve.wait_ge(kb.h("q0"), 3)

            for i in range(NCH):
                ohsl = ohall[:, i * 64:(i + 1) * 64]
                xssl = xslal[:, i * DS:(i + 1) * DS]
                kb.inc("vc", ve.tensor_mul(tmp64[:, :], ohsl, tminr[:, :]))
                kb.wait(ve, "vc")
                kb.inc("vc", ve.tensor_reduce(out=mnv[:, :], in_=tmp64[:, :],
                                              axis=AX.X, op=ALU.add))
                kb.wait(ve, "vc")
                kb.inc("vc", ve.tensor_mul(tmp64[:, :], ohsl, tmaxr[:, :]))
                kb.wait(ve, "vc")
                kb.inc("vc", ve.tensor_reduce(out=mxv[:, :], in_=tmp64[:, :],
                                              axis=AX.X, op=ALU.add))
                kb.wait(ve, "vc")
                kb.inc("vc", ve.tensor_tensor(den[:, :], mxv[:, :], mnv[:, :], ALU.subtract))
                kb.wait(ve, "vc")
                kb.inc("vc", ve.tensor_scalar(out=den[:, :], in0=den[:, :],
                                              scalar1=1e-8, scalar2=None, op0=ALU.add))
                kb.wait(ve, "vc")
                kb.inc("vc", ve.reciprocal(den[:, :], den[:, :]))
                kb.wait(ve, "vc")
                kb.inc("vc", ve.tensor_scalar(out=den[:, :], in0=den[:, :],
                                              scalar1=float(GRID), scalar2=None, op0=ALU.mult))
                kb.wait(ve, "vc")
                kb.inc("vc", ve.tensor_scalar(out=u16[:, :], in0=xssl,
                                              scalar1=mnv[:, 0:1], scalar2=den[:, 0:1],
                                              op0=ALU.subtract, op1=ALU.mult))
                kb.wait(ve, "vc")
                kb.inc("vc", ve.tensor_scalar(out=u16[:, :], in0=u16[:, :],
                                              scalar1=0.0, scalar2=15.999983787536621,
                                              op0=ALU.max, op1=ALU.min))
                kb.wait(ve, "vc")
                kb.inc("vc", ve.tensor_tensor(
                    Cst[:, :].rearrange("p (a b) -> p a b", b=GRID),
                    u16[:, :, None].broadcast_to([128, DS, GRID]),
                    iotg[:, :].rearrange("p (a b) -> p a b", b=GRID), ALU.is_ge))
                kb.wait(ve, "vc")
                kb.inc("vc", ve.tensor_reduce(
                    out=gi_t[:, :], in_=Cst[:, :].rearrange("p (a b) -> p a b", b=GRID),
                    axis=AX.X, op=ALU.add))
                kb.wait(ve, "vc")
                kb.inc("vc", ve.tensor_scalar(out=gi_t[:, :], in0=gi_t[:, :],
                                              scalar1=-1.0, scalar2=None, op0=ALU.add))
                kb.wait(ve, "vc")
                kb.inc("vc", ve.tensor_tensor(xl_t[:, :], u16[:, :], gi_t[:, :], ALU.subtract))
                kb.inc("vc", ve.tensor_tensor(
                    og_t[:, :].rearrange("p (a b) -> p a b", b=GRID),
                    gi_t[:, :, None].broadcast_to([128, DS, GRID]),
                    iotg[:, :].rearrange("p (a b) -> p a b", b=GRID), ALU.is_equal))
                kb.wait(ve, "vc")
                ve.wait_ge(kb.h("q1e"), 2 * (i + 1))
                kb.inc("vc", ve.tensor_mul(bsel[:, :], b_row[:, :], og_t[:, :]))
                kb.wait(ve, "vc")
                kb.inc("vc", ve.tensor_reduce(
                    out=b_v[:, :], in_=bsel[:, :].rearrange("p (a b) -> p a b", b=GRID),
                    axis=AX.X, op=ALU.add))
                kb.wait(ve, "vc")
                kb.inc("vc", ve.tensor_mul(bsel[:, :], s_row[:, :], og_t[:, :]))
                kb.wait(ve, "vc")
                inst = ve.tensor_reduce(
                    out=s_v[:, :], in_=bsel[:, :].rearrange("p (a b) -> p a b", b=GRID),
                    axis=AX.X, op=ALU.add)
                kb.inc("q1z", inst)
                ve.wait_ge(kb.h("q1z"), kb.n("q1z"))
                kb.inc("vc", ve.tensor_tensor(s_v[:, :], s_v[:, :], xl_t[:, :], ALU.mult))
                kb.wait(ve, "vc")
                kb.inc("vc", ve.tensor_add(delta[:, :], s_v[:, :], b_v[:, :]))
                kb.wait(ve, "vc")
                inst = ve.tensor_scalar(out=r1v[:, i:i + 1],
                                        in0=mval[:, 2 * i + 1:2 * i + 2],
                                        scalar1=LN_EPS, scalar2=None, op0=ALU.add)
                inst.then_inc(kb.h("q0"), 1)
                kb.sems["q0"][1] += 1
                ve.wait_ge(kb.h("q1r"), i + 1)
                kb.inc("q1m", ve.tensor_scalar(out=mnv[:, :], in0=mval[:, 2 * i:2 * i + 1],
                                               scalar1=r1v[:, i:i + 1], scalar2=-1.0,
                                               op0=ALU.mult, op1=ALU.mult))
                kb.inc("vc", ve.tensor_mul(
                    sc_t[:, :].rearrange("p (a b) -> p a b", b=DS),
                    ohsl[:, :, None].broadcast_to([128, 64, DS]),
                    delta[:, None, :].broadcast_to([128, 64, DS])))
                kb.wait(ve, "vc")
                ve.wait_ge(kb.h("zr"), i + 1)
                inst = ve.tensor_add(fr_t[:, :], z_t[i % 2][:, :], sc_t[:, :])
                kb.inc("zc", inst)
                ve.wait_ge(kb.h("zc"), i + 1)
                kb.inc("vc", ve.bn_stats(fs6[:, 0:6], fr_t[:, 0:512]))
                kb.inc("vc", ve.bn_stats(fs6[:, 6:12], fr_t[:, 512:1024]))
                kb.inc("vc", ve.bn_stats(fs6[:, 12:18], fr_t[:, 1024:1536]))
                kb.inc("vc", ve.bn_stats(fs6[:, 18:24], fr_t[:, 1536:2048]))
                kb.wait(ve, "vc")
                kb.inc("vc", ve.bn_aggr(fmv[:, :], fs6[:, :]))
                kb.wait(ve, "vc")
                inst = ve.tensor_scalar(out=fr3[:, i:i + 1], in0=fmv[:, 1:2],
                                        scalar1=LN_EPS, scalar2=None, op0=ALU.add)
                inst.then_inc(kb.h("q0"), 1)
                kb.sems["q0"][1] += 1
                ve.wait_ge(kb.h("q1f"), i + 1)
                kb.inc("q1n", ve.tensor_scalar(out=fm3[:, :], in0=fmv[:, 0:1],
                                               scalar1=fr3[:, i:i + 1], scalar2=-1.0,
                                               op0=ALU.mult, op1=ALU.mult))
                if i >= 2:
                    ve.wait_ge(kb.h("ost%d" % (i % 2)), 16 * (i // 2))
                ve.wait_ge(kb.h("q1b"), i + 1)
                inst = ve.scalar_tensor_tensor(
                    out=x2t[i % 2][:, :], in0=fr_t[:, :], scalar=fr3[:, i:i + 1],
                    in1=x1b[i % 2][:, :], op0=ALU.mult, op1=ALU.add)
                kb.inc("q1x", inst)

        @block.scalar
        def _(sc):
            for i in range(NCH):
                sc.wait_ge(kb.h("tc"), i + 1)
                kb.inc("te2", sc.activation(ohfm[:, i * 128:(i + 1) * 128],
                                            ptro[:, :], AF.Copy))
            for i in range(NCH):
                sc.wait_ge(kb.h("q1t"), 2 * i + 1)
                if i >= 1:
                    sc.wait_ge(kb.h("q1z"), i)
                kb.inc("q1e", sc.activation(b_row[:, :], pso[0][:, :], AF.Copy))
                sc.wait_ge(kb.h("q1t"), 2 * i + 2)
                kb.inc("q1e", sc.activation(s_row[:, :], pso[1][:, :], AF.Copy))
                sc.wait_ge(kb.h("q0"), 3 + 2 * i + 1)
                kb.inc("sc", sc.activation(r1v[:, i:i + 1], r1v[:, i:i + 1], AF.Ln))
                kb.wait(sc, "sc")
                kb.inc("q1r", sc.activation(r1v[:, i:i + 1], r1v[:, i:i + 1],
                                            AF.Exp, scale=-0.5))
                sc.wait_ge(kb.h("q1m"), i + 1)
                sc.wait_ge(kb.h("x1l%d" % (i % 2)), 16 * (i // 2 + 1))
                if i >= 2:
                    sc.wait_ge(kb.h("zc"), i - 1)
                kb.inc("zr", sc.activation(z_t[i % 2][:, :], x1b[i % 2][:, :], AF.Identity,
                                           scale=r1v[:, i:i + 1], bias=mnv[:, 0:1]))
                sc.wait_ge(kb.h("q0"), 3 + 2 * i + 2)
                kb.inc("sc", sc.activation(fr3[:, i:i + 1], fr3[:, i:i + 1], AF.Ln))
                kb.wait(sc, "sc")
                kb.inc("q1f", sc.activation(fr3[:, i:i + 1], fr3[:, i:i + 1],
                                            AF.Exp, scale=-0.5))
                sc.wait_ge(kb.h("q1n"), i + 1)
                kb.inc("q1b", sc.activation(x1b[i % 2][:, :], x1b[i % 2][:, :], AF.Identity,
                                            bias=fm3[:, 0:1]))

        @block.tensor
        def _(te):
            te.wait_ge(kb.h("ld"), 16 * NL2)
            for i in range(NCH):
                if i >= 1:
                    te.wait_ge(kb.h("te2"), i)
                kb.inc("tc", te.transpose(ptro[:, :], ohall[:, i * 64:(i + 1) * 64],
                                          identb[:, :]))
            te.wait_ge(kb.h("q0"), 2)
            for i in range(NCH):
                te.wait_ge(kb.h("te2"), i + 1)
                if i >= 1:
                    te.wait_ge(kb.h("q1e"), 2 * i)
                kb.inc("q1t", te.matmul(pso[0][:, :], ohfm[:, i * 128:(i + 1) * 128],
                                        basb[:, :], start=True, stop=True))
                kb.inc("q1t", te.matmul(pso[1][:, :], ohfm[:, i * 128:(i + 1) * 128],
                                        slpb[:, :], start=True, stop=True))

    return nc


_L12 = None
_L2 = None


def kernel(L0_attn_hidden_states, L0_attn_Wq, L0_attn_bq, L0_attn_Wk, L0_attn_bk,
           L0_attn_Wv, L0_attn_bv, L0_attn_Wo, L0_ffn_ln_gamma, L0_ffn_ln_beta,
           L0_ffn_spline_bases, L0_ffn_spline_slopes, L0_ffn_output_scale):
    global _L12, _L2
    x0 = np.ascontiguousarray(np.asarray(L0_attn_hidden_states, np.float32)
                              .reshape(4 * 4096, HID))
    wqt = np.ascontiguousarray(np.asarray(L0_attn_Wq, np.float32).T)
    wkt = np.ascontiguousarray(np.asarray(L0_attn_Wk, np.float32).T)
    wvt = np.ascontiguousarray(np.asarray(L0_attn_Wv, np.float32).T)
    wot = np.ascontiguousarray(np.asarray(L0_attn_Wo, np.float32).T)
    bq_pc = np.ascontiguousarray(np.asarray(L0_attn_bq, np.float32).reshape(16, 128).T)
    bk_row = np.asarray(L0_attn_bk, np.float32).reshape(1, KVD)
    bv_row = np.asarray(L0_attn_bv, np.float32).reshape(1, KVD)
    gam = np.asarray(L0_ffn_ln_gamma, np.float32).reshape(1, HID)
    bet = np.asarray(L0_ffn_ln_beta, np.float32).reshape(1, HID)
    bases2 = np.ascontiguousarray(np.asarray(L0_ffn_spline_bases, np.float32)
                                  .reshape(NT, DS * GRID))
    slopes2 = np.ascontiguousarray(np.asarray(L0_ffn_spline_slopes, np.float32)
                                   .reshape(NT, DS * GRID))
    osc = np.ascontiguousarray(np.asarray(L0_ffn_output_scale, np.float32).reshape(NT, 1))
    gb2 = np.ascontiguousarray(np.concatenate(
        [gam.reshape(NT, DS), bet.reshape(NT, DS)], axis=1))
    nonce = int(np.random.randint(1, 2**30))

    if _L12 is None:
        _L12 = build_l12()
    in_maps_a = []
    for c in range(NCORE):
        in_maps_a.append({
            "x0": x0[c * T:(c + 1) * T],
            "wqt": wqt, "wkt": wkt, "wvt": wvt,
            "bq_pc": bq_pc, "bk_row": bk_row, "bv_row": bv_row,
            "pr1": np.array([[c % 2]], np.int32),
            "nn1": np.array([[nonce]], np.int32),
            "wot": wot, "gam_row": gam, "bet_row": bet, "slopes2": slopes2,
            "gb2": gb2,
        })
    r1b = run_bass_kernel_spmd(_L12, in_maps_a, core_ids=list(range(NCORE)))

    statall = np.ascontiguousarray(
        np.concatenate([r1b.results[c]["st_out"] for c in range(NCORE)], axis=1))

    if _L2 is None:
        _L2 = build_l2()
    in_maps2 = []
    for c in range(NCORE):
        in_maps2.append({
            "x1": r1b.results[c]["x1_out"],
            "oh": r1b.results[c]["oh_out"],
            "xsl": r1b.results[c]["xsl_out"],
            "mv": r1b.results[c]["mv_out"],
            "statall": statall,
            "bases2": bases2, "slopes2": slopes2, "oscale": osc,
        })
    r2 = run_bass_kernel_spmd(_L2, in_maps2, core_ids=list(range(NCORE)))

    out = np.concatenate([r2.results[c]["out"] for c in range(NCORE)], axis=0)
    return out.reshape(4, 4096, HID)


# version 15 (final)
# speedup vs baseline: 1.5499x; 1.5499x over previous
import numpy as np
import ml_dtypes
from contextlib import ExitStack

import concourse.bass as bass
import concourse.mybir as mybir
from concourse.bass_utils import run_bass_kernel_spmd

F32 = mybir.dt.float32
BF16 = mybir.dt.bfloat16
I32 = mybir.dt.int32
U32 = mybir.dt.uint32
U8 = mybir.dt.uint8
AF = mybir.ActivationFunctionType
ALU = mybir.AluOpType
AX = mybir.AxisListType

NCORE = 8
T = 2048
NCH = 16
HID = 2048
H = 16
KVH = 4
DH = 128
KVD = 512
EPS_ATTN = 1e-6
LN_EPS = 1e-5
BIG = 1.0e30
CBIG = 65536.0
NT = 64
DS = 32
GRID = 16
bfdt = ml_dtypes.bfloat16


class KB:
    def __init__(self, nc, ctx, prefix=""):
        self.nc = nc
        self.ctx = ctx
        self.prefix = prefix
        self.sems = {}

    def sem(self, *names):
        for name in names:
            h = self.ctx.enter_context(self.nc.semaphore(self.prefix + name))
            self.sems[name] = [h, 0]

    def inc(self, name, inst, k=1):
        s = self.sems[name]
        inst.then_inc(s[0], k)
        s[1] += k
        return s[1]

    def n(self, name):
        return self.sems[name][1]

    def h(self, name):
        return self.sems[name][0]

    def wait(self, eng, name, val=None):
        eng.wait_ge(self.sems[name][0], self.sems[name][1] if val is None else val)


def build_l12(dbg=False):
    nc = bass.Bass("TRN2", target_bir_lowering=False)
    dt = lambda n, s, d=F32, k="ExternalInput": nc.dram_tensor(n, s, d, kind=k)
    dbg_kv = dt("dbg_kv", [128, 1032], F32, "ExternalOutput").ap() if dbg else None
    dbg_rr = dt("dbg_rr", [1, 4], I32, "ExternalOutput").ap() if dbg else None
    dbg_at = dt("dbg_at", [128, HID], BF16, "ExternalOutput").ap() if dbg else None
    dbg_a0 = dt("dbg_a0", [128, HID], F32, "ExternalOutput").ap() if dbg else None
    dbg_sq = dt("dbg_sq", [128, HID], BF16, "ExternalOutput").ap() if dbg else None

    x0_d = dt("x0", [T, HID]).ap()
    wqt_d = dt("wqt", [HID, HID]).ap()
    wkt_d = dt("wkt", [HID, KVD]).ap()
    wvt_d = dt("wvt", [HID, KVD]).ap()
    bq_d = dt("bq_pc", [128, 16]).ap()
    bk_d = dt("bk_row", [1, KVD]).ap()
    bv_d = dt("bv_row", [1, KVD]).ap()
    pr_d = dt("pr1", [1, 1], I32).ap()
    nn_d = dt("nn1", [1, 1], I32).ap()
    wot_d = dt("wot", [HID, HID]).ap()
    gam_d = dt("gam_row", [1, HID]).ap()
    bet_d = dt("bet_row", [1, HID]).ap()
    slp_d = dt("slopes2", [NT, DS * GRID]).ap()
    gb2_d = dt("gb2", [NT, 64]).ap()

    x1_o = dt("x1_out", [T, HID], F32, "ExternalOutput").ap()
    oh_o = dt("oh_out", [T, NT], BF16, "ExternalOutput").ap()
    xsl_o = dt("xsl_out", [T, DS], F32, "ExternalOutput").ap()
    mv_o = dt("mv_out", [128, 32], F32, "ExternalOutput").ap()
    st_o = dt("st_out", [128, 1], F32, "ExternalOutput").ap()

    sq_scr = nc.dram_tensor("sq_scr", [128, H * T], BF16, kind="Internal")
    ss_scr = nc.dram_tensor("ss_scr", [NT, DS], F32, kind="Internal")
    KVS = nc.dram_tensor("KVS", [128, 2 * 516], F32, addr_space="Shared")
    FLG = nc.dram_tensor("FLG", [1, 16], I32, addr_space="Shared")
    sq_d = sq_scr.ap()

    identb_d = nc.inline_tensor(np.eye(128, dtype=bfdt), "identb_c").ap()
    identf_d = nc.inline_tensor(np.eye(128, dtype=np.float32), "identf_c").ap()
    iot64_d = nc.inline_tensor(np.arange(64, dtype=np.float32)[None, :], "iot64_c").ap()
    iot8_d = nc.inline_tensor(np.arange(8, dtype=np.float32)[None, :], "iot8_c").ap()

    NJ = NCH + 8
    MAXIT = 2000000

    ctx = ExitStack()
    with ctx:
        E = ctx.enter_context
        kba = KB(nc, ctx, "a_")
        kbb = KB(nc, ctx, "b_")
        kba.sem("ldc", "ld0", "ld1", "gld", "gw0", "gw1", "gw2", "gw3", "vc", "sc",
                "p1v", "p2t", "p2s", "kvs", "kbr", "els", "txp0", "txp1",
                "p4w0", "p4w1", "p4t", "p4s", "p4v", "sqs",
                "gin", "kvx", "flw", "plf", "kvp")
        kbb.sem("ldc", "vc", "sc", "scs", "ssb", "kvb_ok",
                "sql0", "sql1", "sql2", "sql3",
                "nmt", "nrs", "dv", "p5t", "atx0", "atx1", "p7s", "p7t", "p7v",
                "x0l0", "x0l1", "x1s0", "x1s1",
                "p8v", "p8s", "p8m", "p8n", "p8t1", "p8e1", "p8t2", "p8e2",
                "p8t3", "p8r", "fin", "stf")
        kb = kba

        sb = lambda n, s, d: E(nc.sbuf_tensor(n, s, d, side="left"))
        bqc = sb("bqc", [128, 16], F32)
        nbqc = sb("nbqc", [128, 16], F32)
        kvloc = sb("kvloc", [128, 516], F32)
        stat6 = sb("stat6", [128, 24], F32)
        mvec = sb("mvec", [128, 2], F32)
        rvec = sb("rvec", [128, 1], F32)
        prb = sb("prb", [1, 1], I32)
        nnb = sb("nnb", [1, 1], I32)
        flb = sb("flb", [1, 2], I32)

        rg = lambda n: E(nc.gpsimd.register(n))
        rpar, rnn, rexp, rcnd, rrem = rg("rpar"), rg("rnn"), rg("rexp"), rg("rcnd"), rg("rrem")
        fv0, fv1, fe0, fe1, rall, rd_ = rg("fv0"), rg("fv1"), rg("fe0"), rg("fe1"), rg("rall"), rg("rd_")

        block = E(nc.Block())

        s_xin = ExitStack()
        xin_fm = s_xin.enter_context(nc.sbuf_tensor("xin_fm", [128, NCH * T], BF16, side="left", is_transpose_dest=True))
        c12 = ExitStack()
        E2 = c12.enter_context
        x0t = [E2(nc.sbuf_tensor(f"ax0t{j}", [128, HID], F32, side="right")) for j in range(2)]
        xint = [E2(nc.sbuf_tensor(f"xint{j}", [128, HID], BF16, side="right")) for j in range(2)]
        wkt_sb = E2(nc.sbuf_tensor("wkt_sb", [128, NCH * KVD], BF16, side="right"))
        wvt_sb = E2(nc.sbuf_tensor("wvt_sb", [128, NCH * KVD], BF16, side="right"))
        bkrep = E2(nc.sbuf_tensor("bkrep", [128, KVD], F32, side="right"))
        bvrep = E2(nc.sbuf_tensor("bvrep", [128, KVD], F32, side="right"))
        kbf = E2(nc.sbuf_tensor("kbf", [128, KVD], F32, side="right"))
        t1f = E2(nc.sbuf_tensor("t1f", [128, KVD], F32, side="right"))
        t2f = E2(nc.sbuf_tensor("t2f", [128, KVD], F32, side="right"))
        t3f = E2(nc.sbuf_tensor("t3f", [128, KVD], F32, side="right"))
        sk_t = E2(nc.sbuf_tensor("sk_t", [128, KVD], BF16, side="right"))
        vext = E2(nc.sbuf_tensor("vext", [128, 516], BF16, side="right"))
        pk = E2(nc.psum_tensor("pk", [128, KVD], F32))
        pv = E2(nc.psum_tensor("pv", [128, KVD], F32))
        pkv = [E2(nc.psum_tensor(f"pkv{j}", [128, 129], F32)) for j in range(4)]

        NLD0 = 3

        @block.sync
        def _(sy):
            kb.inc("ldc", sy.dma_start(out=bqc[:, :], in_=bq_d), 16)
            kb.inc("ldc", sy.dma_start(out=bkrep[:, :], in_=bk_d.broadcast_to([128, KVD])), 16)
            kb.inc("ldc", sy.dma_start(out=bvrep[:, :], in_=bv_d.broadcast_to([128, KVD])), 16)
            assert kb.n("ldc") == 16 * NLD0
            for i in range(NCH + 1):
                if i < NCH:
                    if i >= 2:
                        sy.wait_ge(kb.h("p1v"), 2 * (i - 1))
                    kb.inc("ld%d" % (i % 2), sy.dma_start(out=x0t[i % 2][:, :],
                                                          in_=x0_d[i * 128:(i + 1) * 128, :]), 16)
                if i >= 1:
                    j0 = i - 1
                    sy.wait_ge(kb.h("p1v"), 2 * (j0 + 1))
                    kb.inc("txp%d" % (j0 % 2), sy.dma_start(
                        out=xin_fm[:, :].rearrange("p (j t) -> p j t", t=T)[:, :, j0 * 128:(j0 + 1) * 128],
                        in_=xint[j0 % 2][:, :].rearrange("p (j d) -> p j d", d=128),
                        transpose=True), 16)

        @block.gpsimd
        def _(g):
            g.dma_start(out=prb[:, :], in_=pr_d).then_inc(kba.h("gin"), 16)
            g.dma_start(out=nnb[:, :], in_=nn_d).then_inc(kba.h("gin"), 16)
            g.wait_ge(kba.h("gin"), 32)
            g.reg_load(rpar, prb[0:1, 0:1])
            g.reg_load(rnn, nnb[0:1, 0:1])
            for q4 in range(4):
                kb.inc("gw%d" % q4, g.dma_start(
                    out=wkt_sb[:, q4 * 4 * KVD:(q4 + 1) * 4 * KVD].rearrange("p (dc o) -> p dc o", o=KVD),
                    in_=wkt_d.rearrange("(dc p) o -> p dc o", p=128)[:, q4 * 4:(q4 + 1) * 4, :]), 16)
                kb.inc("gw%d" % q4, g.dma_start(
                    out=wvt_sb[:, q4 * 4 * KVD:(q4 + 1) * 4 * KVD].rearrange("p (dc o) -> p dc o", o=KVD),
                    in_=wvt_d.rearrange("(dc p) o -> p dc o", p=128)[:, q4 * 4:(q4 + 1) * 4, :]), 16)

        @block.vector
        def _(ve):
            def skstuff(i):
                ve.wait_ge(kb.h("p2t"), 3 * i + 1)
                kb.inc("kbr", ve.tensor_add(kbf[:, :], pk[:, :], bkrep[:, :]))
                ve.wait_ge(kb.h("p2t"), 3 * i + 2)
                inst = ve.tensor_add(
                    vext[:, :].rearrange("p (a b) -> p a b", b=129)[:, :, 0:128],
                    pv[:, :].rearrange("p (a b) -> p a b", b=128),
                    bvrep[:, :].rearrange("p (a b) -> p a b", b=128))
                kb.inc("kvs", inst)
                ve.wait_ge(kb.h("els"), 2 * (i + 1))
                inst = ve.tensor_add(sk_t[:, :], t2f[:, :], t3f[:, :])
                kb.inc("kvs", inst)

            ve.wait_ge(kb.h("ldc"), 16 * NLD0)
            kb.inc("vc", ve.tensor_scalar(out=nbqc[:, :], in0=bqc[:, :], scalar1=-1.0,
                                          scalar2=None, op0=ALU.mult))
            kb.inc("vc", ve.memset(vext[:, :], 1.0))
            for i in range(NCH):
                ve.wait_ge(kb.h("ld%d" % (i % 2)), 16 * (i // 2 + 1))
                xc = x0t[i % 2]
                kb.inc("vc", ve.bn_stats(stat6[:, 0:6], xc[:, 0:512]))
                kb.inc("vc", ve.bn_stats(stat6[:, 6:12], xc[:, 512:1024]))
                kb.inc("vc", ve.bn_stats(stat6[:, 12:18], xc[:, 1024:1536]))
                kb.inc("vc", ve.bn_stats(stat6[:, 18:24], xc[:, 1536:2048]))
                kb.wait(ve, "vc")
                kb.inc("vc", ve.bn_aggr(mvec[:, :], stat6[:, :]))
                kb.wait(ve, "vc")
                kb.inc("p1v", ve.tensor_scalar(out=rvec[:, :], in0=mvec[:, 1:2],
                                               scalar1=LN_EPS, scalar2=None, op0=ALU.add))
                ve.wait_ge(kb.h("p2s"), i + 1)
                if i >= 2:
                    ve.wait_ge(kb.h("txp%d" % (i % 2)), 16 * (i // 2))
                kb.inc("p1v", ve.tensor_scalar(
                    out=xint[i % 2][:, :], in0=xc[:, :], scalar1=mvec[:, 0:1],
                    scalar2=rvec[:, 0:1], op0=ALU.subtract, op1=ALU.mult))
                if i >= 1:
                    skstuff(i - 1)
            skstuff(NCH - 1)
            ve.wait_ge(kb.h("p2t"), 3 * NCH)
            kb.inc("vc", ve.tensor_copy(kvloc[:, 0:129], pkv[0][:, :]))
            kb.inc("vc", ve.tensor_copy(kvloc[:, 129:258], pkv[1][:, :]))
            kb.inc("vc", ve.tensor_copy(kvloc[:, 258:387], pkv[2][:, :]))
            inst = ve.tensor_copy(kvloc[:, 387:516], pkv[3][:, :])
            kb.inc("kvs", inst)

        @block.scalar
        def _(sc):
            def elu_k(i):
                sc.wait_ge(kb.h("kbr"), i + 1)
                if i >= 1:
                    sc.wait_ge(kb.h("kvs"), 2 * i)
                kb.inc("sc", sc.activation(t1f[:, :], kbf[:, :], AF.Relu, scale=-1.0))
                kb.inc("els", sc.activation(t3f[:, :], kbf[:, :], AF.Relu))
                kb.wait(sc, "sc")
                kb.inc("els", sc.activation(t2f[:, :], t1f[:, :], AF.Exp, scale=-1.0))

            for i in range(NCH):
                sc.wait_ge(kb.h("p1v"), 2 * i + 1)
                kb.inc("sc", sc.activation(rvec[:, :], rvec[:, :], AF.Ln))
                kb.wait(sc, "sc")
                kb.inc("p2s", sc.activation(rvec[:, :], rvec[:, :], AF.Exp, scale=-0.5))
                if i >= 1:
                    elu_k(i - 1)
            elu_k(NCH - 1)

        @block.tensor
        def _(te):
            te.wait_ge(kb.h("ldc"), 16 * NLD0)
            for i in range(NCH):
                te.wait_ge(kb.h("txp%d" % (i % 2)), 16 * (i // 2 + 1))
                if i >= 1:
                    te.wait_ge(kb.h("kbr"), i)
                    te.wait_ge(kb.h("kvs"), 2 * i - 1)
                for dc in range(NCH):
                    if i == 0:
                        te.wait_ge(kb.h("gw%d" % (dc // 4)), 32)
                    inst = te.matmul(pk[:, :], xin_fm[:, dc * T + i * 128:dc * T + (i + 1) * 128],
                                     wkt_sb[:, dc * KVD:(dc + 1) * KVD],
                                     start=(dc == 0), stop=(dc == NCH - 1))
                kb.inc("p2t", inst)
                for dc in range(NCH):
                    inst = te.matmul(pv[:, :], xin_fm[:, dc * T + i * 128:dc * T + (i + 1) * 128],
                                     wvt_sb[:, dc * KVD:(dc + 1) * KVD],
                                     start=(dc == 0), stop=(dc == NCH - 1))
                kb.inc("p2t", inst)
                te.wait_ge(kb.h("kvs"), 2 * i + 2)
                for kv4 in range(KVH):
                    inst = te.matmul(pkv[kv4][:, :],
                                     sk_t[:, kv4 * 128:(kv4 + 1) * 128],
                                     vext[:, kv4 * 129:(kv4 + 1) * 129],
                                     start=(i == 0), stop=(i == NCH - 1))
                kb.inc("p2t", inst)

        c12.close()
        c4 = ExitStack()
        E4 = c4.enter_context
        sq_fm = E4(nc.sbuf_tensor("sq_fm", [128, NCH * T], BF16, side="right"))
        wqb = [E4(nc.sbuf_tensor(f"wqb{j}", [128, NCH * 128], BF16, side="right")) for j in range(2)]
        qa1 = E4(nc.sbuf_tensor("qa1", [128, 512], F32, side="right"))
        qa2 = E4(nc.sbuf_tensor("qa2", [128, 512], F32, side="right"))
        qa3 = E4(nc.sbuf_tensor("qa3", [128, 512], F32, side="right"))
        pq = [E4(nc.psum_tensor(f"pq{j}", [128, 512], F32)) for j in range(2)]

        @block.gpsimd
        def _(g):
            g.wait_ge(kb.h("kvs"), 2 * NCH + 1)
            g.wait_ge(kb.h("p2t"), 3 * NCH)
            g.wait_ge(kb.h("txp0"), 16 * (NCH // 2))
            g.wait_ge(kb.h("txp1"), 16 * (NCH // 2))
            with g.If_eq(rpar, 0):
                g.dma_start(out=KVS.ap()[:, 0:516], in_=kvloc[:, :]).then_inc(kba.h("kvx"), 16)
            with g.Else():
                g.dma_start(out=KVS.ap()[:, 516:1032], in_=kvloc[:, :]).then_inc(kba.h("kvx"), 16)
            g.wait_ge(kba.h("kvx"), 16)
            with g.If_eq(rpar, 0):
                g.dma_start(out=FLG.ap()[0:1, 0:1], in_=nnb[0:1, 0:1]).then_inc(kba.h("flw"), 16)
            with g.Else():
                g.dma_start(out=FLG.ap()[0:1, 1:2], in_=nnb[0:1, 0:1]).then_inc(kba.h("flw"), 16)
            g.wait_ge(kba.h("flw"), 16)
            for oc in range(16):
                if oc >= 2:
                    g.wait_ge(kb.h("p4t"), 4 * (oc - 1))
                kb.inc("p4w%d" % (oc % 2), g.dma_start(
                    out=wqb[oc % 2][:, :].rearrange("p (dc o) -> p dc o", o=128),
                    in_=wqt_d.rearrange("(dc p) o -> p dc o", p=128)[:, :, oc * 128:(oc + 1) * 128]), 16)

        @block.tensor
        def _(te):
            for oc in range(16):
                te.wait_ge(kb.h("p4w%d" % (oc % 2)), 16 * (oc // 2 + 1))
                for tl in range(4):
                    n4 = oc * 4 + tl
                    if n4 >= 2:
                        te.wait_ge(kb.h("p4v"), n4 - 1)
                    for dc in range(NCH):
                        inst = te.matmul(pq[n4 % 2][:, :],
                                         wqb[oc % 2][:, dc * 128:(dc + 1) * 128],
                                         xin_fm[:, dc * T + tl * 512:dc * T + (tl + 1) * 512],
                                         start=(dc == 0), stop=(dc == NCH - 1))
                    kb.inc("p4t", inst)

        @block.scalar
        def _(sc):
            for oc in range(16):
                for tl in range(4):
                    n4 = oc * 4 + tl
                    sc.wait_ge(kb.h("p4t"), n4 + 1)
                    if n4 >= 1:
                        sc.wait_ge(kb.h("p4s"), 2 * n4)
                        sc.wait_ge(kb.h("p4v"), n4)
                    kb.inc("sc", sc.activation(qa1[:, :], pq[n4 % 2][:, :], AF.Relu,
                                               scale=-1.0, bias=nbqc[:, oc:oc + 1]))
                    kb.inc("p4s", sc.activation(qa3[:, :], pq[n4 % 2][:, :], AF.Relu,
                                                bias=bqc[:, oc:oc + 1]))
                    kb.wait(sc, "sc")
                    kb.inc("p4s", sc.activation(qa2[:, :], qa1[:, :], AF.Exp, scale=-1.0))

        @block.vector
        def _(ve):
            for oc in range(16):
                for tl in range(4):
                    n4 = oc * 4 + tl
                    ve.wait_ge(kb.h("p4s"), 2 * n4 + 2)
                    kb.inc("p4v", ve.tensor_add(
                        sq_fm[:, oc * T + tl * 512:oc * T + (tl + 1) * 512],
                        qa2[:, :], qa3[:, :]))

        @block.sync
        def _(sy):
            for oc in range(16):
                sy.wait_ge(kb.h("p4v"), 4 * (oc + 1))
                kb.inc("sqs", sy.dma_start(
                    out=sq_d.rearrange("p (i ht) -> p i ht", ht=H * 128)[:, :, oc * 128:(oc + 1) * 128],
                    in_=sq_fm[:, oc * T:(oc + 1) * T].rearrange("p (i t) -> p i t", t=128)), 16)
            sy.wait_ge(kb.h("sqs"), 16 * 16)

        s_xin.close()
        c4.close()

        kb = kbb

        sb = lambda n, s, d: E(nc.sbuf_tensor(n, s, d, side="left"))
        identb = sb("identb", [128, 128], BF16)
        identf = sb("identf", [128, 128], F32)
        kvb = sb("kvb", [128, 516], BF16)
        slp_sb = sb("slp_sb", [64, DS * GRID], F32)
        ssd = sb("ssd", [64, DS], F32)
        gb2s = sb("gb2s", [64, 64], F32)
        nrc = sb("nrc", [128, 32], F32)
        epsb = sb("epsb", [128, 1], F32)
        vz = sb("vz", [128, 1], F32)
        r2v = sb("r2v", [128, 1], F32)
        Rv = sb("Rv", [128, 1], F32)
        mv7 = sb("mv7", [128, 2], F32)
        rv7 = sb("rv7", [128, 1], F32)
        mr7n = sb("mr7n", [128, 1], F32)
        m1v1 = sb("m1v1", [128, 32], F32)
        statrun = sb("statrun", [128, 1], F32)
        tokmn = sb("tokmn", [128, 1], F32)
        tokmx = sb("tokmx", [128, 1], F32)
        csc = sb("csc", [128, 8], F32)
        mx8 = sb("mx8", [128, 8], F32)
        idxu = sb("idxu", [128, 8], U32)
        idxf = sb("idxf", [128, 8], F32)
        ocoh = sb("ocoh", [128, 8], F32)
        mskd = sb("mskd", [128, 64], F32)
        msk8 = sb("msk8", [128, 64], U8)
        tsc = sb("tsc", [128, 64], F32)
        bigt = sb("bigt", [128, 64], F32)
        iot64 = sb("iot64", [128, 64], F32)
        iot8 = sb("iot8", [128, 8], F32)
        selmm = [sb(f"selmm{j}", [128, 128], F32) for j in range(2)]
        colst = sb("colst", [128, 1], F32)
        ohTs = sb("ohTs", [64, 128], F32)
        gamsel = sb("gamsel", [128, 64], F32)
        wsel = sb("wsel", [128, 8 * DS], F32)
        xslst = sb("xslst", [128, NCH * DS], F32)
        ohstore = sb("ohstore", [128, NCH * 64], BF16)
        kvpart = sb("kvpart", [128, 516], F32)
        sqc = [sb(f"sqc{j}", [128, H * 128], BF16) for j in range(4)]
        attnr = [sb(f"attnr{j}", [128, HID], BF16) for j in range(2)]

        rb = lambda n, s, d: E(nc.sbuf_tensor(n, s, d, side="right"))
        wot_sb = rb("wot_sb", [128, NCH * HID], BF16)
        afm = [E(nc.sbuf_tensor(f"afm{j}", [128, HID], BF16, side="right",
                                is_transpose_dest=True)) for j in range(2)]
        x0t = [rb(f"x0t{j}", [128, HID], F32) for j in range(2)]
        aout = rb("aout", [128, HID], F32)
        x1t = [rb(f"x1t{j}", [128, HID], F32) for j in range(2)]
        w_t = rb("w_t", [128, HID], F32)
        scr = rb("scr", [128, HID], F32)
        y_t = rb("y_t", [128, HID], BF16)
        gsrep = rb("gsrep", [128, HID], F32)
        nbsrep = rb("nbsrep", [128, HID], F32)
        ohf32 = rb("ohf32", [128, 8 * 64], F32)

        pq5 = [E(nc.psum_tensor(f"pq5_{j}", [128, 512], F32)) for j in range(4)]
        paux = E(nc.psum_tensor("paux", [128, 512], F32))
        po7 = [E(nc.psum_tensor(f"po7_{j}", [128, 512], F32)) for j in range(2)]
        ptr8 = E(nc.psum_tensor("ptr8", [128, 128], F32))

        NLD = 8

        @block.sync
        def _(sy):
            for t_, d_ in [(identb, identb_d), (identf, identf_d),
                           (slp_sb, slp_d), (gb2s, gb2_d)]:
                kb.inc("ldc", sy.dma_start(out=t_[:, :], in_=d_), 16)
            kb.inc("ldc", sy.dma_start(out=iot64[:, :], in_=iot64_d.broadcast_to([128, 64])), 16)
            kb.inc("ldc", sy.dma_start(out=iot8[:, :], in_=iot8_d.broadcast_to([128, 8])), 16)
            kb.inc("ldc", sy.dma_start(out=w_t[:, :], in_=gam_d.broadcast_to([128, HID])), 16)
            kb.inc("ldc", sy.dma_start(out=scr[:, :], in_=bet_d.broadcast_to([128, HID])), 16)
            kb.inc("sql0", sy.dma_start(out=sqc[0][:, :], in_=sq_d[:, 0:2048]), 16)
            kb.inc("sql1", sy.dma_start(out=sqc[1][:, :], in_=sq_d[:, 2048:4096]), 16)
            assert kb.n("ldc") == 16 * NLD
            for j in range(NCH + 1):
                if j < NCH:
                    if j >= 2:
                        sy.wait_ge(kb.h("p8v"), 4 * (j - 2) + 3)
                    kb.inc("x0l%d" % (j % 2), sy.dma_start(
                        out=x0t[j % 2][:, :], in_=x0_d[j * 128:(j + 1) * 128, :]), 16)
                    sy.wait_ge(kb.h("dv"), 16 * (j + 1))
                    if j >= 2:
                        sy.wait_ge(kb.h("p7t"), 4 * (j - 1))
                    kb.inc("atx%d" % (j % 2), sy.dma_start(
                        out=afm[j % 2][:, :].rearrange("p (a b) -> p a b", b=128),
                        in_=attnr[j % 2][:, :].rearrange("p (a b) -> p a b", b=128),
                        transpose=True), 16)
                    if j + 2 < NCH:
                        if j >= 2:
                            sy.wait_ge(kb.h("p5t"), j - 1)
                        kb.inc("sql%d" % ((j + 2) % 4), sy.dma_start(
                            out=sqc[(j + 2) % 4][:, :],
                            in_=sq_d[:, (j + 2) * 2048:(j + 3) * 2048]), 16)
                if j >= 1:
                    c = j - 1
                    sy.wait_ge(kb.h("p8v"), 4 * c + 3)
                    kb.inc("x1s%d" % (c % 2), sy.dma_start(
                        out=x1_o[c * 128:(c + 1) * 128, :], in_=x1t[c % 2][:, :]), 16)

        @block.gpsimd
        def _(g):
            g.wait_ge(kba.h("sqs"), 256)
            g.wait_ge(kba.h("p4t"), 64)
            g.wait_ge(kba.h("p4v"), 64)
            g.reg_mov(rexp, 16)
            g.reg_mov(rcnd, 1)
            g.reg_mov(rrem, MAXIT)
            with g.While(rcnd):
                g.dma_start(out=flb[0:1, 0:2], in_=FLG.ap()[0:1, 0:2]).then_inc(kba.h("plf"), 16)
                g.wait_ge(kba.h("plf"), rexp)
                g.reg_add(rexp, rexp, 16)
                g.reg_load(fv0, flb[0:1, 0:1])
                g.reg_load(fv1, flb[0:1, 1:2])
                g.reg_alu(fe0, fv0, rnn, ALU.is_equal)
                g.reg_alu(fe1, fv1, rnn, ALU.is_equal)
                g.reg_alu(rall, fe0, fe1, ALU.mult)
                g.reg_sub(rd_, 1, rall)
                g.reg_sub(rrem, rrem, 1)
                g.reg_alu(rcnd, rd_, rrem, ALU.mult)
            with g.If_eq(rpar, 0):
                g.dma_start(out=kvpart[:, :], in_=KVS.ap()[:, 516:1032]).then_inc(kba.h("kvp"), 16)
            with g.Else():
                g.dma_start(out=kvpart[:, :], in_=KVS.ap()[:, 0:516]).then_inc(kba.h("kvp"), 16)
            if dbg:
                g.reg_save(dbg_rr[0:1, 0:1], rrem)
                g.reg_save(dbg_rr[0:1, 1:2], fv0)
                g.reg_save(dbg_rr[0:1, 2:3], fv1)
                g.reg_save(dbg_rr[0:1, 3:4], rnn)
                g.wait_ge(kba.h("kvp"), 16)
                g.dma_start(out=dbg_kv[:, 0:516], in_=kvloc[:, :]).then_inc(kba.h("gin"), 16)
                g.dma_start(out=dbg_kv[:, 516:1032], in_=kvpart[:, :]).then_inc(kba.h("gin"), 16)
                g.wait_ge(kba.h("gin"), 64)
            kb.inc("p7s", g.dma_start(
                out=wot_sb[:, :].rearrange("p (ac h) -> p ac h", h=HID),
                in_=wot_d.rearrange("(ac p) h -> p ac h", p=128)), 16)
            g.wait_ge(kb.h("scs"), 1)
            kb.inc("ssb", g.dma_start(out=ss_scr.ap(), in_=ssd[:, :]), 16)
            g.wait_ge(kb.h("ssb"), 16)
            kb.inc("ssb", g.dma_start(
                out=y_t[:, :],
                in_=ss_scr.ap().rearrange("a b -> (a b)")[None, :].broadcast_to([128, HID])), 16)
            if dbg:
                g.wait_ge(kb.h("sql1"), 32)
                g.dma_start(out=dbg_sq, in_=sqc[1][:, :]).then_inc(kba.h("gin"), 16)
                g.wait_ge(kb.h("dv"), 16 * 6)
                g.dma_start(out=dbg_at, in_=attnr[1][:, :]).then_inc(kba.h("gin"), 16)
                g.wait_ge(kb.h("p7v"), 4 * 5 + 4)
                g.dma_start(out=dbg_a0, in_=aout[:, :]).then_inc(kba.h("gin"), 16)
                g.wait_ge(kba.h("gin"), 112)

        @block.vector
        def _(ve):
            ve.wait_ge(kb.h("ldc"), 16 * NLD)
            ve.wait_ge(kba.h("kvp"), 16)
            ve.wait_ge(kba.h("sqs"), 256)
            kb.inc("vc", ve.tensor_add(kvloc[:, :], kvloc[:, :], kvpart[:, :]))
            kb.wait(ve, "vc")
            kb.inc("kvb_ok", ve.tensor_copy(kvb[:, :], kvloc[:, :]))
            kb.inc("vc", ve.tensor_reduce(
                out=ssd[:, :], in_=slp_sb[:, :].rearrange("p (a b) -> p a b", b=GRID),
                axis=AX.X, op=ALU.add))
            kb.wait(ve, "vc")
            kb.inc("vc", ve.tensor_scalar(out=ssd[:, :], in0=ssd[:, :],
                                          scalar1=1.0 / GRID, scalar2=None, op0=ALU.mult))
            kb.inc("vc", ve.memset(epsb[:, :], EPS_ATTN))
            kb.inc("vc", ve.memset(bigt[:, :], BIG))
            kb.inc("vc", ve.memset(statrun[:, :], BIG))
            ve.wait_ge(kb.h("ssb"), 32)
            kb.inc("vc", ve.tensor_mul(gsrep[:, :], w_t[:, :], y_t[:, :]))
            inst = ve.scalar_tensor_tensor(
                out=nbsrep[:, :], in0=scr[:, :], scalar=-1.0, in1=y_t[:, :],
                op0=ALU.mult, op1=ALU.mult)
            kb.inc("vc", inst)
            ve.wait_ge(kb.h("vc"), kb.n("vc"))

        @block.scalar
        def _(sc):
            sc.wait_ge(kb.h("vc"), 6)
            kb.inc("scs", sc.activation(ssd[:, :], ssd[:, :], AF.Sign))

        @block.tensor
        def _(te):
            te.wait_ge(kb.h("ldc"), 16 * NLD)
            te.wait_ge(kb.h("kvb_ok"), 1)
            te.wait_ge(kb.h("p7s"), 16)
            for j in range(NJ):
                if 0 <= j - 2 < NCH:
                    c = j - 2
                    te.wait_ge(kb.h("atx%d" % (c % 2)), 16 * (c // 2 + 1))
                    for hc in range(4):
                        n7 = c * 4 + hc
                        if n7 >= 2:
                            te.wait_ge(kb.h("p7v"), n7 - 1)
                        for ac in range(NCH):
                            inst = te.matmul(po7[n7 % 2][:, :],
                                             afm[c % 2][:, ac * 128:(ac + 1) * 128],
                                             wot_sb[:, ac * HID + hc * 512:ac * HID + (hc + 1) * 512],
                                             start=(ac == 0), stop=(ac == NCH - 1))
                        kb.inc("p7t", inst)
                if j < NCH:
                    te.wait_ge(kb.h("sql%d" % (j % 4)), 16 * (j // 4 + 1))
                    if j >= 2:
                        te.wait_ge(kb.h("nrs"), j - 1)
                    for h_ in range(16):
                        inst = te.matmul(
                            paux[:, (j % 2) * 16 + h_:(j % 2) * 16 + h_ + 1],
                            sqc[j % 4][:, h_ * 128:(h_ + 1) * 128],
                            kvb[:, (h_ // 4) * 129 + 128:(h_ // 4) * 129 + 129],
                            start=True, stop=True)
                    kb.inc("nmt", inst)
                    if j >= 1:
                        te.wait_ge(kb.h("dv"), 16 * j)
                    for h_ in range(16):
                        inst = te.matmul(
                            pq5[h_ % 4][:, (h_ // 4) * 128:(h_ // 4) * 128 + 128],
                            sqc[j % 4][:, h_ * 128:(h_ + 1) * 128],
                            kvb[:, (h_ // 4) * 129:(h_ // 4) * 129 + 128],
                            start=True, stop=True)
                    kb.inc("p5t", inst)
                if 0 <= j - 4 < NCH:
                    c = j - 4
                    te.wait_ge(kb.h("p8m"), c + 1)
                    if c >= 1:
                        te.wait_ge(kb.h("p8e1"), c)
                    inst = te.transpose(paux[0:64, 96:224],
                                        ohf32[:, (c % 8) * 64:(c % 8) * 64 + 64],
                                        identf[:, :])
                    kb.inc("p8t1", inst)
                if 0 <= j - 5 < NCH:
                    c = j - 5
                    te.wait_ge(kb.h("p8e1"), c + 1)
                    if c >= 1:
                        te.wait_ge(kb.h("p8e2"), c)
                    inst = te.matmul(paux[:, 32:96], ohTs[:, :], gb2s[:, :],
                                     start=True, stop=True)
                    kb.inc("p8t2", inst)
                if 0 <= j - 7 < NCH:
                    c = j - 7
                    te.wait_ge(kb.h("p8n"), c + 1)
                    if c >= 1:
                        te.wait_ge(kb.h("p8r"), c)
                    inst = te.transpose(ptr8[:, :], selmm[c % 2][:, :], identf[:, :])
                    kb.inc("p8t3", inst)

        @block.scalar
        def _(sc):
            for j in range(NJ):
                if 0 <= j - 1 < NCH:
                    c = j - 1
                    if c >= 1:
                        sc.wait_ge(kb.h("p8v"), 4 * (c - 1) + 3)
                    for hc in range(4):
                        sc.wait_ge(kb.h("p7t"), 4 * c + hc + 1)
                        kb.inc("p7v", sc.activation(aout[:, hc * 512:(hc + 1) * 512],
                                                    po7[(c * 4 + hc) % 2][:, :], AF.Copy))
                    sc.wait_ge(kb.h("p8v"), 4 * c + 1)
                    kb.inc("sc", sc.activation(rv7[:, :], rv7[:, :], AF.Ln))
                    kb.wait(sc, "sc")
                    kb.inc("p8s", sc.activation(rv7[:, :], rv7[:, :], AF.Exp, scale=-0.5))
                if j < NCH:
                    sc.wait_ge(kb.h("nmt"), j + 1)
                    sc.wait_ge(kb.h("vc"), 6)
                    nsl = nrc[:, (j % 2) * 16:(j % 2) * 16 + 16]
                    kb.inc("sc", sc.activation(nsl, paux[:, (j % 2) * 16:(j % 2) * 16 + 16],
                                               AF.Ln, bias=epsb[:, 0:1]))
                    kb.wait(sc, "sc")
                    kb.inc("nrs", sc.activation(nsl, nsl, AF.Exp, scale=-1.0))
                    sc.wait_ge(kb.h("nrs"), j + 1)
                    sc.wait_ge(kb.h("p5t"), j + 1)
                    if j >= 2:
                        sc.wait_ge(kb.h("atx%d" % (j % 2)), 16 * (j // 2))
                    for h_ in range(16):
                        kb.inc("dv", sc.activation(
                            attnr[j % 2][:, h_ * 128:(h_ + 1) * 128],
                            pq5[h_ % 4][:, (h_ // 4) * 128:(h_ // 4) * 128 + 128],
                            AF.Copy, scale=nrc[:, (j % 2) * 16 + h_:(j % 2) * 16 + h_ + 1]))
                if 0 <= j - 1 < NCH:
                    c = j - 1
                    sc.wait_ge(kb.h("p8v"), 4 * c + 4)
                    kb.inc("sc", sc.activation(rvec[:, :], rvec[:, :], AF.Ln))
                    kb.wait(sc, "sc")
                    kb.inc("p8s", sc.activation(rvec[:, :], rvec[:, :], AF.Exp, scale=-0.5))
                    kb.inc("sc", sc.activation(r2v[:, :], vz[:, :], AF.Ln))
                    kb.wait(sc, "sc")
                    kb.inc("p8s", sc.activation(r2v[:, :], r2v[:, :], AF.Exp, scale=-0.5))
                if 0 <= j - 4 < NCH:
                    c = j - 4
                    sc.wait_ge(kb.h("p8t1"), c + 1)
                    if c >= 1:
                        sc.wait_ge(kb.h("p8t2"), c)
                    kb.inc("p8e1", sc.activation(ohTs[:, :], paux[0:64, 96:224], AF.Copy))
                if 0 <= j - 5 < NCH:
                    c = j - 5
                    sc.wait_ge(kb.h("p8t2"), c + 1)
                    if c >= 1:
                        sc.wait_ge(kb.h("p8n"), c)
                    kb.inc("p8e2", sc.activation(gamsel[:, :], paux[:, 32:96], AF.Copy))

        @block.vector
        def _(ve):
            for j in range(NJ):
                if 0 <= j - 7 < NCH:
                    c = j - 7
                    ve.wait_ge(kb.h("p8t3"), c + 1)
                    kb.inc("vc", ve.tensor_reduce(out=colst[:, :], in_=ptr8[:, :],
                                                  axis=AX.X, op=ALU.min))
                    kb.wait(ve, "vc")
                    if c >= 1:
                        ve.wait_ge(kb.h("p8r"), c)
                    inst = ve.tensor_tensor(statrun[:, :], statrun[:, :], colst[:, :], ALU.min)
                    kb.inc("p8r", inst)
                    if c == NCH - 1:
                        kb.wait(ve, "vc")
                        ve.wait_ge(kb.h("p8r"), NCH)
                        kb.inc("fin", ve.tensor_copy(ohstore[:, 0:1], ohstore[:, 0:1]))
                        kb.inc("fin", ve.tensor_copy(xslst[:, 0:1], xslst[:, 0:1]))
                        kb.inc("fin", ve.tensor_copy(m1v1[:, 0:1], m1v1[:, 0:1]))
                        kb.inc("fin", ve.tensor_copy(statrun[:, 0:1], statrun[:, 0:1]))
                if 0 <= j - 5 < NCH:
                    c = j - 5
                    ve.wait_ge(kb.h("p8e2"), c + 1)
                    xsl = xslst[:, c * DS:(c + 1) * DS]
                    kb.inc("vc", ve.tensor_mul(xsl, wsel[:, (c % 8) * DS:(c % 8 + 1) * DS],
                                               gamsel[:, 0:32]))
                    kb.wait(ve, "vc")
                    kb.inc("vc", ve.tensor_add(xsl, xsl, gamsel[:, 32:64]))
                    kb.wait(ve, "vc")
                    kb.inc("vc", ve.tensor_reduce(out=tokmn[:, :], in_=xsl,
                                                  axis=AX.X, op=ALU.min))
                    kb.inc("vc", ve.tensor_reduce(out=tokmx[:, :], in_=xsl,
                                                  axis=AX.X, op=ALU.max))
                    kb.wait(ve, "vc")
                    kb.inc("vc", ve.tensor_scalar(out=tokmx[:, :], in0=tokmx[:, :],
                                                  scalar1=-1.0, scalar2=None, op0=ALU.mult))
                    kb.wait(ve, "vc")
                    kb.inc("vc", ve.tensor_copy(msk8[:, :], ohstore[:, c * 64:(c + 1) * 64]))
                    kb.wait(ve, "vc")
                    if c >= 2:
                        ve.wait_ge(kb.h("p8t3"), c - 1)
                    kb.inc("vc", ve.select(selmm[c % 2][:, 0:64], msk8[:, :],
                                           tokmn[:, :].broadcast_to([128, 64]), bigt[:, :]))
                    kb.wait(ve, "vc")
                    inst = ve.select(selmm[c % 2][:, 64:128], msk8[:, :],
                                     tokmx[:, :].broadcast_to([128, 64]), bigt[:, :])
                    kb.inc("p8n", inst)
                if j < NCH:
                    ve.wait_ge(kb.h("p7v"), 4 * j + 1)
                    kb.inc("vc", ve.bn_stats(stat6[:, 0:6], aout[:, 0:512]))
                    ve.wait_ge(kb.h("p7v"), 4 * j + 2)
                    kb.inc("vc", ve.bn_stats(stat6[:, 6:12], aout[:, 512:1024]))
                    ve.wait_ge(kb.h("p7v"), 4 * j + 3)
                    kb.inc("vc", ve.bn_stats(stat6[:, 12:18], aout[:, 1024:1536]))
                    ve.wait_ge(kb.h("p7v"), 4 * (j + 1))
                    kb.inc("vc", ve.bn_stats(stat6[:, 18:24], aout[:, 1536:2048]))
                    kb.wait(ve, "vc")
                    kb.inc("vc", ve.bn_aggr(mv7[:, :], stat6[:, :]))
                    kb.wait(ve, "vc")
                    kb.inc("p8v", ve.tensor_scalar(out=rv7[:, :], in0=mv7[:, 1:2],
                                                   scalar1=LN_EPS, scalar2=None, op0=ALU.add))
                    ve.wait_ge(kb.h("p8s"), 3 * j + 1)
                    kb.inc("p8v", ve.tensor_scalar(out=mr7n[:, :], in0=mv7[:, 0:1],
                                                   scalar1=rv7[:, 0:1], scalar2=-1.0,
                                                   op0=ALU.mult, op1=ALU.mult))
                    ve.wait_ge(kb.h("p8v"), kb.n("p8v"))
                    ve.wait_ge(kb.h("x0l%d" % (j % 2)), 16 * (j // 2 + 1))
                    kb.inc("vc", ve.tensor_scalar(out=x0t[j % 2][:, :], in0=x0t[j % 2][:, :],
                                                  scalar1=mr7n[:, 0:1], scalar2=None,
                                                  op0=ALU.add))
                    kb.wait(ve, "vc")
                    if j >= 2:
                        ve.wait_ge(kb.h("x1s%d" % (j % 2)), 16 * (j // 2))
                    inst = ve.scalar_tensor_tensor(
                        out=x1t[j % 2][:, :], in0=aout[:, :], scalar=rv7[:, 0:1],
                        in1=x0t[j % 2][:, :], op0=ALU.mult, op1=ALU.add)
                    kb.inc("p8v", inst)
                    ve.wait_ge(kb.h("p8v"), 4 * j + 3)
                    x1c = x1t[j % 2]
                    kb.inc("vc", ve.bn_stats(stat6[:, 0:6], x1c[:, 0:512]))
                    kb.inc("vc", ve.bn_stats(stat6[:, 6:12], x1c[:, 512:1024]))
                    kb.inc("vc", ve.bn_stats(stat6[:, 12:18], x1c[:, 1024:1536]))
                    kb.inc("vc", ve.bn_stats(stat6[:, 18:24], x1c[:, 1536:2048]))
                    kb.wait(ve, "vc")
                    kb.inc("vc", ve.bn_aggr(mvec[:, :], stat6[:, :]))
                    kb.wait(ve, "vc")
                    kb.inc("vc", ve.tensor_copy(m1v1[:, 2 * j:2 * j + 2], mvec[:, :]))
                    kb.inc("vc", ve.tensor_scalar(out=rvec[:, :], in0=mvec[:, 1:2],
                                                  scalar1=LN_EPS, scalar2=None, op0=ALU.add))
                    kb.wait(ve, "vc")
                    kb.inc("vc", ve.reciprocal(vz[:, :], rvec[:, :]))
                    kb.wait(ve, "vc")
                    kb.inc("vc", ve.tensor_tensor(vz[:, :], vz[:, :], mvec[:, 1:2], ALU.mult))
                    kb.wait(ve, "vc")
                    inst = ve.tensor_scalar(out=vz[:, :], in0=vz[:, :],
                                            scalar1=LN_EPS, scalar2=None, op0=ALU.add)
                    kb.inc("p8v", inst)
                    ve.wait_ge(kb.h("p8s"), 3 * j + 3)
                    kb.inc("vc", ve.tensor_tensor(Rv[:, :], rvec[:, :], r2v[:, :], ALU.mult))
                    kb.wait(ve, "vc")
                    kb.inc("vc", ve.tensor_scalar(out=w_t[:, :], in0=x1c[:, :],
                                                  scalar1=mvec[:, 0:1], scalar2=Rv[:, 0:1],
                                                  op0=ALU.subtract, op1=ALU.mult))
                    kb.wait(ve, "vc")
                    kb.inc("vc", ve.tensor_mul(scr[:, :], w_t[:, :], gsrep[:, :]))
                    kb.wait(ve, "vc")
                    kb.inc("vc", ve.tensor_tensor(y_t[:, :], scr[:, :], nbsrep[:, :], ALU.is_ge))
                    kb.wait(ve, "vc")
                    kb.inc("vc", ve.tensor_reduce(
                        out=tsc[:, :], in_=y_t[:, :].rearrange("p (a b) -> p a b", b=DS),
                        axis=AX.X, op=ALU.add))
                    kb.wait(ve, "vc")
                    kb.inc("vc", ve.tensor_scalar(out=tsc[:, :], in0=tsc[:, :],
                                                  scalar1=2.0, scalar2=-32.0,
                                                  op0=ALU.mult, op1=ALU.add))
                    kb.wait(ve, "vc")
                    kb.inc("vc", ve.tensor_reduce(
                        out=csc[:, :], in_=tsc[:, :].rearrange("p (a b) -> p a b", b=8),
                        axis=AX.X, op=ALU.add))
                    kb.wait(ve, "vc")
                    kb.inc("vc", ve.max(mx8[:, :], csc[:, :]))
                    kb.wait(ve, "vc")
                    kb.inc("vc", ve.max_index(idxu[:, :], mx8[:, :], csc[:, :]))
                    kb.wait(ve, "vc")
                    kb.inc("vc", ve.tensor_copy(idxf[:, :], idxu[:, :]))
                    kb.wait(ve, "vc")
                    kb.inc("vc", ve.tensor_scalar(out=ocoh[:, :], in0=iot8[:, :],
                                                  scalar1=idxf[:, 0:1], scalar2=None,
                                                  op0=ALU.is_equal))
                    kb.wait(ve, "vc")
                    kb.inc("vc", ve.scalar_tensor_tensor(
                        out=mskd[:, :],
                        in0=ocoh[:, :, None].broadcast_to([128, 8, 8]),
                        scalar=CBIG, in1=tsc[:, :], op0=ALU.mult, op1=ALU.add))
                    kb.wait(ve, "vc")
                    kb.inc("vc", ve.max(mx8[:, :], mskd[:, :]))
                    kb.wait(ve, "vc")
                    kb.inc("vc", ve.max_index(idxu[:, :], mx8[:, :], mskd[:, :]))
                    kb.wait(ve, "vc")
                    kb.inc("vc", ve.tensor_copy(idxf[:, :], idxu[:, :]))
                    kb.wait(ve, "vc")
                    ohsl = ohstore[:, j * 64:(j + 1) * 64]
                    kb.inc("vc", ve.tensor_scalar(
                        out=ohf32[:, (j % 8) * 64:(j % 8) * 64 + 64], in0=iot64[:, :],
                        scalar1=idxf[:, 0:1], scalar2=None, op0=ALU.is_equal))
                    inst = ve.tensor_scalar(out=ohsl, in0=iot64[:, :],
                                            scalar1=idxf[:, 0:1], scalar2=None,
                                            op0=ALU.is_equal)
                    kb.inc("p8m", inst)
                    ve.wait_ge(kb.h("p8m"), j + 1)
                    kb.inc("vc", ve.tensor_mul(
                        scr[:, :].rearrange("p (a b) -> p a b", b=DS),
                        w_t[:, :].rearrange("p (a b) -> p a b", b=DS),
                        ohsl[:, :, None].broadcast_to([128, 64, DS])))
                    kb.wait(ve, "vc")
                    kb.inc("vc", ve.tensor_reduce(
                        out=wsel[:, (j % 8) * DS:(j % 8 + 1) * DS],
                        in_=scr[:, :].rearrange("p (t i) -> p i t", i=DS),
                        axis=AX.X, op=ALU.add))
                    kb.wait(ve, "vc")

        @block.sync
        def _(sy):
            sy.wait_ge(kb.h("fin"), 4)
            sy.wait_ge(kb.h("x1s0"), 16 * (NCH // 2))
            sy.wait_ge(kb.h("x1s1"), 16 * (NCH // 2))
            kb.inc("stf", sy.dma_start(out=oh_o.rearrange("(i p) t -> p i t", p=128),
                                       in_=ohstore[:, :].rearrange("p (i t) -> p i t", t=64)), 16)
            kb.inc("stf", sy.dma_start(out=xsl_o.rearrange("(i p) t -> p i t", p=128),
                                       in_=xslst[:, :].rearrange("p (i t) -> p i t", t=DS)), 16)
            kb.inc("stf", sy.dma_start(out=mv_o, in_=m1v1[:, :]), 16)
            kb.inc("stf", sy.dma_start(out=st_o, in_=statrun[:, :]), 16)
            sy.wait_ge(kb.h("stf"), kb.n("stf"))
    return nc


def build_l2():
    nc = bass.Bass("TRN2", target_bir_lowering=False)
    dt = lambda n, s, d=F32, k="ExternalInput": nc.dram_tensor(n, s, d, kind=k)

    x1_d = dt("x1", [T, HID]).ap()
    oh_d = dt("oh", [T, NT], BF16).ap()
    xsl_d = dt("xsl", [T, DS]).ap()
    mv_d = dt("mv", [128, 32]).ap()
    sta_d = dt("statall", [128, 8]).ap()
    bas_d = dt("bases2", [NT, DS * GRID]).ap()
    slp_d = dt("slopes2", [NT, DS * GRID]).ap()
    osc_d = dt("oscale", [NT, 1]).ap()
    out_d = dt("out", [T, HID], F32, "ExternalOutput").ap()
    st_scr = nc.dram_tensor("st_scr", [128, 1], F32, kind="Internal")

    identb_d = nc.inline_tensor(np.eye(128, dtype=bfdt), "identb_c").ap()
    iotg_d = nc.inline_tensor(
        np.tile(np.arange(GRID, dtype=np.float32), DS)[None, :], "iotg_c").ap()

    with ExitStack() as ctx:
        E = ctx.enter_context
        kb = KB(nc, ctx)
        kb.sem("ld", "stb", "ost0", "ost1", "vc", "sc", "tc", "te2", "q0", "q0s",
               "q1t", "q1e", "q1r", "q1f", "q1x", "q1z", "x1l0", "x1l1", "fin",
               "q1m", "zr", "zc", "q1n", "q1b")
        sbt = lambda n, s, d: E(nc.sbuf_tensor(n, s, d))

        identb = sbt("identb", [128, 128], BF16)
        iotg = sbt("iotg", [128, DS * GRID], F32)
        statall = sbt("sta_sb", [128, 8], F32)
        statc = sbt("statc", [128, 1], F32)
        tminr = sbt("tminr", [128, 64], F32)
        tmaxr = sbt("tmaxr", [128, 64], F32)
        bas = sbt("bas", [64, DS * GRID], F32)
        slp = sbt("slp", [64, DS * GRID], F32)
        osc = sbt("osc", [64, 1], F32)
        basb = sbt("basb", [64, DS * GRID], BF16)
        slpb = sbt("slpb", [64, DS * GRID], BF16)
        ohall = sbt("ohall", [128, NCH * 64], BF16)
        ohfm = sbt("ohfm", [64, T], BF16)
        xslal = sbt("xslal", [128, NCH * DS], F32)
        mval = sbt("mval", [128, 32], F32)
        x1b = [sbt(f"x1b{j}", [128, HID], F32) for j in range(2)]
        r1v = sbt("r1v", [128, 32], F32)
        fr3 = sbt("fr3", [128, 32], F32)
        z_t = [sbt(f"z_t{j}", [128, HID], F32) for j in range(2)]
        b_row = sbt("b_row", [128, DS * GRID], F32)
        s_row = sbt("s_row", [128, DS * GRID], F32)
        mnv = sbt("mnv", [128, 1], F32)
        mxv = sbt("mxv", [128, 1], F32)
        den = sbt("den", [128, 1], F32)
        u_t = sbt("u_t", [128, DS], F32)
        u16 = sbt("u16", [128, DS], F32)
        Cst = sbt("Cst", [128, DS * GRID], F32)
        gi_t = sbt("gi_t", [128, DS], F32)
        xl_t = sbt("xl_t", [128, DS], F32)
        og_t = sbt("og_t", [128, DS * GRID], BF16)
        bsel = sbt("bsel", [128, DS * GRID], F32)
        b_v = sbt("b_v", [128, DS], F32)
        s_v = sbt("s_v", [128, DS], F32)
        delta = sbt("delta", [128, DS], F32)
        sc_t = sbt("sc_t", [128, HID], F32)
        fr_t = sbt("fr_t", [128, HID], F32)
        fs6 = sbt("fs6", [128, 24], F32)
        fmv = sbt("fmv", [128, 2], F32)
        fm3 = sbt("fm3", [128, 1], F32)
        x2t = [sbt(f"x2t{j}", [128, HID], F32) for j in range(2)]
        tmp64 = sbt("tmp64", [128, 64], F32)
        pso = [E(nc.psum_tensor(f"pso{j}", [128, 512], F32)) for j in range(2)]
        ptro = E(nc.psum_tensor("ptro", [64, 128], BF16))

        block = E(nc.Block())
        NL2 = 9

        @block.sync
        def _(sy):
            for t_, d_ in [(identb, identb_d), (statall, sta_d), (bas, bas_d),
                           (slp, slp_d), (osc, osc_d), (mval, mv_d)]:
                kb.inc("ld", sy.dma_start(out=t_[:, :], in_=d_), 16)
            kb.inc("ld", sy.dma_start(out=iotg[:, :],
                                      in_=iotg_d.broadcast_to([128, DS * GRID])), 16)
            kb.inc("ld", sy.dma_start(
                out=ohall[:, :].rearrange("p (i t) -> p i t", t=64),
                in_=oh_d.rearrange("(i p) t -> p i t", p=128)), 16)
            kb.inc("ld", sy.dma_start(
                out=xslal[:, :].rearrange("p (i t) -> p i t", t=DS),
                in_=xsl_d.rearrange("(i p) t -> p i t", p=128)), 16)
            assert kb.n("ld") == 16 * NL2
            sy.wait_ge(kb.h("q0"), 1)
            kb.inc("stb", sy.dma_start(out=st_scr.ap(), in_=statc[:, :]), 16)
            sy.wait_ge(kb.h("stb"), 16)
            kb.inc("stb", sy.dma_start(
                out=tminr[:, :],
                in_=st_scr.ap().rearrange("a b -> (a b)")[None, 0:64].broadcast_to([128, 64])), 16)
            kb.inc("stb", sy.dma_start(
                out=tmaxr[:, :],
                in_=st_scr.ap().rearrange("a b -> (a b)")[None, 64:128].broadcast_to([128, 64])), 16)
            sy.wait_ge(kb.h("stb"), 48)
            sy.sem_inc(kb.h("q0s"), 1)
            kb.sems["q0s"][1] += 1
            for i in range(NCH + 2):
                if i < NCH:
                    if i >= 2:
                        sy.wait_ge(kb.h("q1x"), i - 1)
                    kb.inc("x1l%d" % (i % 2), sy.dma_start(
                        out=x1b[i % 2][:, :], in_=x1_d[i * 128:(i + 1) * 128, :]), 16)
                if i >= 2:
                    j = i - 2
                    sy.wait_ge(kb.h("q1x"), j + 1)
                    kb.inc("ost%d" % (j % 2), sy.dma_start(
                        out=out_d[j * 128:(j + 1) * 128, :], in_=x2t[j % 2][:, :]), 16)
            sy.wait_ge(kb.h("ost0"), 16 * (NCH // 2))
            sy.wait_ge(kb.h("ost1"), 16 * (NCH // 2))

        @block.vector
        def _(ve):
            ve.wait_ge(kb.h("ld"), 16 * NL2)
            inst = ve.tensor_reduce(out=statc[:, :], in_=statall[:, :],
                                    axis=AX.X, op=ALU.min)
            inst.then_inc(kb.h("q0"), 1)
            kb.sems["q0"][1] += 1
            kb.inc("vc", ve.tensor_scalar(out=bas[:, :], in0=bas[:, :],
                                          scalar1=osc[:, 0:1], scalar2=None, op0=ALU.mult))
            kb.inc("vc", ve.tensor_scalar(out=slp[:, :], in0=slp[:, :],
                                          scalar1=osc[:, 0:1], scalar2=None, op0=ALU.mult))
            kb.wait(ve, "vc")
            kb.inc("vc", ve.tensor_copy(basb[:, :], bas[:, :]))
            inst = ve.tensor_copy(slpb[:, :], slp[:, :])
            inst.then_inc(kb.h("q0"), 1)
            kb.sems["q0"][1] += 1
            ve.wait_ge(kb.h("q0s"), 1)
            inst = ve.tensor_scalar(out=tmaxr[:, :], in0=tmaxr[:, :],
                                    scalar1=-1.0, scalar2=None, op0=ALU.mult)
            inst.then_inc(kb.h("q0"), 1)
            kb.sems["q0"][1] += 1
            ve.wait_ge(kb.h("q0"), 3)

            for i in range(NCH):
                ohsl = ohall[:, i * 64:(i + 1) * 64]
                xssl = xslal[:, i * DS:(i + 1) * DS]
                kb.inc("vc", ve.tensor_mul(tmp64[:, :], ohsl, tminr[:, :]))
                kb.wait(ve, "vc")
                kb.inc("vc", ve.tensor_reduce(out=mnv[:, :], in_=tmp64[:, :],
                                              axis=AX.X, op=ALU.add))
                kb.wait(ve, "vc")
                kb.inc("vc", ve.tensor_mul(tmp64[:, :], ohsl, tmaxr[:, :]))
                kb.wait(ve, "vc")
                kb.inc("vc", ve.tensor_reduce(out=mxv[:, :], in_=tmp64[:, :],
                                              axis=AX.X, op=ALU.add))
                kb.wait(ve, "vc")
                kb.inc("vc", ve.tensor_tensor(den[:, :], mxv[:, :], mnv[:, :], ALU.subtract))
                kb.wait(ve, "vc")
                kb.inc("vc", ve.tensor_scalar(out=den[:, :], in0=den[:, :],
                                              scalar1=1e-8, scalar2=None, op0=ALU.add))
                kb.wait(ve, "vc")
                kb.inc("vc", ve.reciprocal(den[:, :], den[:, :]))
                kb.wait(ve, "vc")
                kb.inc("vc", ve.tensor_scalar(out=den[:, :], in0=den[:, :],
                                              scalar1=float(GRID), scalar2=None, op0=ALU.mult))
                kb.wait(ve, "vc")
                kb.inc("vc", ve.tensor_scalar(out=u16[:, :], in0=xssl,
                                              scalar1=mnv[:, 0:1], scalar2=den[:, 0:1],
                                              op0=ALU.subtract, op1=ALU.mult))
                kb.wait(ve, "vc")
                kb.inc("vc", ve.tensor_scalar(out=u16[:, :], in0=u16[:, :],
                                              scalar1=0.0, scalar2=15.999983787536621,
                                              op0=ALU.max, op1=ALU.min))
                kb.wait(ve, "vc")
                kb.inc("vc", ve.tensor_tensor(
                    Cst[:, :].rearrange("p (a b) -> p a b", b=GRID),
                    u16[:, :, None].broadcast_to([128, DS, GRID]),
                    iotg[:, :].rearrange("p (a b) -> p a b", b=GRID), ALU.is_ge))
                kb.wait(ve, "vc")
                kb.inc("vc", ve.tensor_reduce(
                    out=gi_t[:, :], in_=Cst[:, :].rearrange("p (a b) -> p a b", b=GRID),
                    axis=AX.X, op=ALU.add))
                kb.wait(ve, "vc")
                kb.inc("vc", ve.tensor_scalar(out=gi_t[:, :], in0=gi_t[:, :],
                                              scalar1=-1.0, scalar2=None, op0=ALU.add))
                kb.wait(ve, "vc")
                kb.inc("vc", ve.tensor_tensor(xl_t[:, :], u16[:, :], gi_t[:, :], ALU.subtract))
                kb.inc("vc", ve.tensor_tensor(
                    og_t[:, :].rearrange("p (a b) -> p a b", b=GRID),
                    gi_t[:, :, None].broadcast_to([128, DS, GRID]),
                    iotg[:, :].rearrange("p (a b) -> p a b", b=GRID), ALU.is_equal))
                kb.wait(ve, "vc")
                ve.wait_ge(kb.h("q1e"), 2 * (i + 1))
                kb.inc("vc", ve.tensor_mul(bsel[:, :], b_row[:, :], og_t[:, :]))
                kb.wait(ve, "vc")
                kb.inc("vc", ve.tensor_reduce(
                    out=b_v[:, :], in_=bsel[:, :].rearrange("p (a b) -> p a b", b=GRID),
                    axis=AX.X, op=ALU.add))
                kb.wait(ve, "vc")
                kb.inc("vc", ve.tensor_mul(bsel[:, :], s_row[:, :], og_t[:, :]))
                kb.wait(ve, "vc")
                inst = ve.tensor_reduce(
                    out=s_v[:, :], in_=bsel[:, :].rearrange("p (a b) -> p a b", b=GRID),
                    axis=AX.X, op=ALU.add)
                kb.inc("q1z", inst)
                ve.wait_ge(kb.h("q1z"), kb.n("q1z"))
                kb.inc("vc", ve.tensor_tensor(s_v[:, :], s_v[:, :], xl_t[:, :], ALU.mult))
                kb.wait(ve, "vc")
                kb.inc("vc", ve.tensor_add(delta[:, :], s_v[:, :], b_v[:, :]))
                kb.wait(ve, "vc")
                inst = ve.tensor_scalar(out=r1v[:, i:i + 1],
                                        in0=mval[:, 2 * i + 1:2 * i + 2],
                                        scalar1=LN_EPS, scalar2=None, op0=ALU.add)
                inst.then_inc(kb.h("q0"), 1)
                kb.sems["q0"][1] += 1
                ve.wait_ge(kb.h("q1r"), i + 1)
                kb.inc("q1m", ve.tensor_scalar(out=mnv[:, :], in0=mval[:, 2 * i:2 * i + 1],
                                               scalar1=r1v[:, i:i + 1], scalar2=-1.0,
                                               op0=ALU.mult, op1=ALU.mult))
                kb.inc("vc", ve.tensor_mul(
                    sc_t[:, :].rearrange("p (a b) -> p a b", b=DS),
                    ohsl[:, :, None].broadcast_to([128, 64, DS]),
                    delta[:, None, :].broadcast_to([128, 64, DS])))
                kb.wait(ve, "vc")
                ve.wait_ge(kb.h("zr"), i + 1)
                inst = ve.tensor_add(fr_t[:, :], z_t[i % 2][:, :], sc_t[:, :])
                kb.inc("zc", inst)
                ve.wait_ge(kb.h("zc"), i + 1)
                kb.inc("vc", ve.bn_stats(fs6[:, 0:6], fr_t[:, 0:512]))
                kb.inc("vc", ve.bn_stats(fs6[:, 6:12], fr_t[:, 512:1024]))
                kb.inc("vc", ve.bn_stats(fs6[:, 12:18], fr_t[:, 1024:1536]))
                kb.inc("vc", ve.bn_stats(fs6[:, 18:24], fr_t[:, 1536:2048]))
                kb.wait(ve, "vc")
                kb.inc("vc", ve.bn_aggr(fmv[:, :], fs6[:, :]))
                kb.wait(ve, "vc")
                inst = ve.tensor_scalar(out=fr3[:, i:i + 1], in0=fmv[:, 1:2],
                                        scalar1=LN_EPS, scalar2=None, op0=ALU.add)
                inst.then_inc(kb.h("q0"), 1)
                kb.sems["q0"][1] += 1
                ve.wait_ge(kb.h("q1f"), i + 1)
                kb.inc("q1n", ve.tensor_scalar(out=fm3[:, :], in0=fmv[:, 0:1],
                                               scalar1=fr3[:, i:i + 1], scalar2=-1.0,
                                               op0=ALU.mult, op1=ALU.mult))
                if i >= 2:
                    ve.wait_ge(kb.h("ost%d" % (i % 2)), 16 * (i // 2))
                ve.wait_ge(kb.h("q1b"), i + 1)
                inst = ve.scalar_tensor_tensor(
                    out=x2t[i % 2][:, :], in0=fr_t[:, :], scalar=fr3[:, i:i + 1],
                    in1=x1b[i % 2][:, :], op0=ALU.mult, op1=ALU.add)
                kb.inc("q1x", inst)

        @block.scalar
        def _(sc):
            for i in range(NCH):
                sc.wait_ge(kb.h("tc"), i + 1)
                kb.inc("te2", sc.activation(ohfm[:, i * 128:(i + 1) * 128],
                                            ptro[:, :], AF.Copy))
            for i in range(NCH):
                sc.wait_ge(kb.h("q1t"), 2 * i + 1)
                if i >= 1:
                    sc.wait_ge(kb.h("q1z"), i)
                kb.inc("q1e", sc.activation(b_row[:, :], pso[0][:, :], AF.Copy))
                sc.wait_ge(kb.h("q1t"), 2 * i + 2)
                kb.inc("q1e", sc.activation(s_row[:, :], pso[1][:, :], AF.Copy))
                sc.wait_ge(kb.h("q0"), 3 + 2 * i + 1)
                kb.inc("sc", sc.activation(r1v[:, i:i + 1], r1v[:, i:i + 1], AF.Ln))
                kb.wait(sc, "sc")
                kb.inc("q1r", sc.activation(r1v[:, i:i + 1], r1v[:, i:i + 1],
                                            AF.Exp, scale=-0.5))
                sc.wait_ge(kb.h("q1m"), i + 1)
                sc.wait_ge(kb.h("x1l%d" % (i % 2)), 16 * (i // 2 + 1))
                if i >= 2:
                    sc.wait_ge(kb.h("zc"), i - 1)
                kb.inc("zr", sc.activation(z_t[i % 2][:, :], x1b[i % 2][:, :], AF.Identity,
                                           scale=r1v[:, i:i + 1], bias=mnv[:, 0:1]))
                sc.wait_ge(kb.h("q0"), 3 + 2 * i + 2)
                kb.inc("sc", sc.activation(fr3[:, i:i + 1], fr3[:, i:i + 1], AF.Ln))
                kb.wait(sc, "sc")
                kb.inc("q1f", sc.activation(fr3[:, i:i + 1], fr3[:, i:i + 1],
                                            AF.Exp, scale=-0.5))
                sc.wait_ge(kb.h("q1n"), i + 1)
                kb.inc("q1b", sc.activation(x1b[i % 2][:, :], x1b[i % 2][:, :], AF.Identity,
                                            bias=fm3[:, 0:1]))

        @block.tensor
        def _(te):
            te.wait_ge(kb.h("ld"), 16 * NL2)
            for i in range(NCH):
                if i >= 1:
                    te.wait_ge(kb.h("te2"), i)
                kb.inc("tc", te.transpose(ptro[:, :], ohall[:, i * 64:(i + 1) * 64],
                                          identb[:, :]))
            te.wait_ge(kb.h("q0"), 2)
            for i in range(NCH):
                te.wait_ge(kb.h("te2"), i + 1)
                if i >= 1:
                    te.wait_ge(kb.h("q1e"), 2 * i)
                kb.inc("q1t", te.matmul(pso[0][:, :], ohfm[:, i * 128:(i + 1) * 128],
                                        basb[:, :], start=True, stop=True))
                kb.inc("q1t", te.matmul(pso[1][:, :], ohfm[:, i * 128:(i + 1) * 128],
                                        slpb[:, :], start=True, stop=True))

    return nc


_L12 = None
_L2 = None


def kernel(L0_attn_hidden_states, L0_attn_Wq, L0_attn_bq, L0_attn_Wk, L0_attn_bk,
           L0_attn_Wv, L0_attn_bv, L0_attn_Wo, L0_ffn_ln_gamma, L0_ffn_ln_beta,
           L0_ffn_spline_bases, L0_ffn_spline_slopes, L0_ffn_output_scale):
    global _L12, _L2
    x0 = np.ascontiguousarray(np.asarray(L0_attn_hidden_states, np.float32)
                              .reshape(4 * 4096, HID))
    wqt = np.ascontiguousarray(np.asarray(L0_attn_Wq, np.float32).T)
    wkt = np.ascontiguousarray(np.asarray(L0_attn_Wk, np.float32).T)
    wvt = np.ascontiguousarray(np.asarray(L0_attn_Wv, np.float32).T)
    wot = np.ascontiguousarray(np.asarray(L0_attn_Wo, np.float32).T)
    bq_pc = np.ascontiguousarray(np.asarray(L0_attn_bq, np.float32).reshape(16, 128).T)
    bk_row = np.asarray(L0_attn_bk, np.float32).reshape(1, KVD)
    bv_row = np.asarray(L0_attn_bv, np.float32).reshape(1, KVD)
    gam = np.asarray(L0_ffn_ln_gamma, np.float32).reshape(1, HID)
    bet = np.asarray(L0_ffn_ln_beta, np.float32).reshape(1, HID)
    bases2 = np.ascontiguousarray(np.asarray(L0_ffn_spline_bases, np.float32)
                                  .reshape(NT, DS * GRID))
    slopes2 = np.ascontiguousarray(np.asarray(L0_ffn_spline_slopes, np.float32)
                                   .reshape(NT, DS * GRID))
    osc = np.ascontiguousarray(np.asarray(L0_ffn_output_scale, np.float32).reshape(NT, 1))
    gb2 = np.ascontiguousarray(np.concatenate(
        [gam.reshape(NT, DS), bet.reshape(NT, DS)], axis=1))
    nonce = int(np.random.randint(1, 2**30))

    if _L12 is None:
        _L12 = build_l12()
    in_maps_a = []
    for c in range(NCORE):
        in_maps_a.append({
            "x0": x0[c * T:(c + 1) * T],
            "wqt": wqt, "wkt": wkt, "wvt": wvt,
            "bq_pc": bq_pc, "bk_row": bk_row, "bv_row": bv_row,
            "pr1": np.array([[c % 2]], np.int32),
            "nn1": np.array([[nonce]], np.int32),
            "wot": wot, "gam_row": gam, "bet_row": bet, "slopes2": slopes2,
            "gb2": gb2,
        })
    r1b = run_bass_kernel_spmd(_L12, in_maps_a, core_ids=list(range(NCORE)))

    statall = np.ascontiguousarray(
        np.concatenate([r1b.results[c]["st_out"] for c in range(NCORE)], axis=1))

    if _L2 is None:
        _L2 = build_l2()
    in_maps2 = []
    for c in range(NCORE):
        in_maps2.append({
            "x1": r1b.results[c]["x1_out"],
            "oh": r1b.results[c]["oh_out"],
            "xsl": r1b.results[c]["xsl_out"],
            "mv": r1b.results[c]["mv_out"],
            "statall": statall,
            "bases2": bases2, "slopes2": slopes2, "oscale": osc,
        })
    r2 = run_bass_kernel_spmd(_L2, in_maps2, core_ids=list(range(NCORE)))

    out = np.concatenate([r2.results[c]["out"] for c in range(NCORE)], axis=0)
    return out.reshape(4, 4096, HID)
